# version 01 (final)
# baseline (speedup 1.0000x reference)
import math
import numpy as np
from contextlib import ExitStack

import concourse.bass as bass
import concourse.mybir as mybir
from concourse.bass_utils import run_bass_kernel_spmd
from concourse.alu_op_type import AluOpType as OP

F32 = mybir.dt.float32
BF16 = mybir.dt.bfloat16
AF = mybir.ActivationFunctionType

NSEQ = 2
L = 2048
D = 2048
NT = NSEQ * L
TB = 512
NDC = D // 128
HEADS = 16
HD = 128
INTER = 5632
NIC = INTER // 128
BLK = 256
NBLK = L // BLK
EPS0 = 1e-5
EPS1 = 1e-6
SCALE0 = HD ** -0.5
RSQRT8 = 1.0 / math.sqrt(8.0)
ENGS = ("sync", "tensor", "vector", "scalar", "gpsimd")


class SemC:
    __slots__ = ("h", "n", "name")

    def __init__(self, h, name):
        self.h = h
        self.n = 0
        self.name = name

    def inc(self, k=1):
        self.n += k
        return self.n


class Em:

    def __init__(self, P):
        self.P = P
        self.q = {e: [] for e in ENGS}

    def on(self, eng, fn):
        self.q[eng].append(fn)

    def flush(self):
        for eng in ENGS:
            fns = self.q[eng]
            if not fns:
                continue

            def runner(e, fns=fns):
                for fn in fns:
                    fn(e)
            getattr(self.P.block, eng)(runner)
        self.q = {e: [] for e in ENGS}


class Prog:
    def __init__(self, nc, es, block):
        self.nc = nc
        self.es = es
        self.block = block
        self.sems = {}
        self.nbar = 0
        self.dram = {}
        self.scr_v = es.enter_context(nc.sbuf_tensor("scr_v", [1, 8], F32))
        self.scr_s = es.enter_context(nc.sbuf_tensor("scr_s", [1, 8], F32))
        self.scr_g = es.enter_context(nc.sbuf_tensor("scr_g", [1, 8], F32))
        self.bar = self.sem("bar")
        init = self.sem("init")

        @block.vector
        def _(v):
            v.memset(self.scr_s[:], 0.0).then_inc(init.h, 1)
        init.inc()

        @block.scalar
        def _(s):
            s.wait_ge(init.h, 1)

    def sem(self, name):
        if name not in self.sems:
            h = self.es.enter_context(self.nc.semaphore(name))
            self.sems[name] = SemC(h, name)
        return self.sems[name]

    def dsem(self, name):
        return self.sem(f"{name}.{self.nbar}")

    def barrier(self):
        self.nbar += 1
        tgt = 5 * self.nbar
        bar = self.bar
        dma_waits = [(s.h, s.n) for nm, s in self.sems.items()
                     if (nm.startswith("ld") or nm.startswith("st")) and s.n > 0]

        @self.block.sync
        def _(q):
            for h, v in dma_waits:
                q.wait_ge(h, v)
            q.nop().then_inc(bar.h, 1)
            q.wait_ge(bar.h, tgt)

        @self.block.tensor
        def _(t):
            t.nop().then_inc(bar.h, 1)
            t.wait_ge(bar.h, tgt)

        @self.block.vector
        def _(v):
            v.memset(self.scr_v[:], 0.0).then_inc(bar.h, 1)
            v.wait_ge(bar.h, tgt)

        @self.block.scalar
        def _(s):
            s.activation(out=self.scr_s[:], in_=self.scr_s[:], func=AF.Copy).then_inc(bar.h, 1)
            s.wait_ge(bar.h, tgt)

        @self.block.gpsimd
        def _(g):
            g.memset(self.scr_g[:], 0.0).then_inc(bar.h, 1)
            g.wait_ge(bar.h, tgt)


def _fm_in(dram2d, c0=None, c1=None):
    ap = dram2d if c0 is None else dram2d[:, c0:c1]
    return ap.rearrange("(c p) w -> p c w", p=128)


def _t3(tile, c, w):
    return tile[:].rearrange("p (c w) -> p c w", c=c)



def ph_pre(P, es):
    nc = P.nc
    t = P.t = {}
    t["ones_row"] = es.enter_context(nc.sbuf_tensor("ones_row", [1, 128], BF16))
    t["ones_col"] = es.enter_context(nc.sbuf_tensor("ones_col", [128, 1], BF16))
    t["ident"] = es.enter_context(nc.sbuf_tensor("ident", [128, 128], BF16))
    t["iid"] = es.enter_context(nc.sbuf_tensor("iid", [128, 128], F32))
    for nm in ("bq_c", "bk_c", "bo_c", "fb_c", "nw_c", "ds_c", "slope_nb"):
        t[nm] = es.enter_context(nc.sbuf_tensor("sb_" + nm, [128, 16], F32))
    t["slope_b"] = es.enter_context(nc.sbuf_tensor("sb_slope_b", [128, 16], F32))
    t["gs_c"] = es.enter_context(nc.sbuf_tensor("sb_gs_c", [128, 44], F32))
    t["us_c"] = es.enter_context(nc.sbuf_tensor("sb_us_c", [128, 44], F32))
    t["bv_row"] = es.enter_context(nc.sbuf_tensor("sb_bv_row", [1, 2048], F32))
    t["bv_rb"] = es.enter_context(nc.sbuf_tensor("sb_bv_rb", [1, 2048], BF16))
    t["eps0"] = es.enter_context(nc.sbuf_tensor("sb_eps0", [128, 1], F32))
    t["eps1"] = es.enter_context(nc.sbuf_tensor("sb_eps1", [128, 1], F32))

    ld = P.dsem("ld_pre")
    em = Em(P)

    def loads(q):
        q.dma_start(out=t["bq_c"][:], in_=P.dram["bq_c"]).then_inc(ld.h, 16)
        q.dma_start(out=t["bk_c"][:], in_=P.dram["bk_c"]).then_inc(ld.h, 16)
        q.dma_start(out=t["bo_c"][:], in_=P.dram["bo_c"]).then_inc(ld.h, 16)
        q.dma_start(out=t["fb_c"][:], in_=P.dram["fb_c"]).then_inc(ld.h, 16)
        q.dma_start(out=t["nw_c"][:], in_=P.dram["nw_c"]).then_inc(ld.h, 16)
        q.dma_start(out=t["ds_c"][:], in_=P.dram["ds_c"]).then_inc(ld.h, 16)
        q.dma_start(out=t["slope_b"][:], in_=P.dram["slope_b"]).then_inc(ld.h, 16)
        q.dma_start(out=t["gs_c"][:], in_=P.dram["gs_c"]).then_inc(ld.h, 16)
        q.dma_start(out=t["us_c"][:], in_=P.dram["us_c"]).then_inc(ld.h, 16)
        q.dma_start(out=t["bv_row"][:], in_=P.dram["bv_row"]).then_inc(ld.h, 16)
    ld.inc(160)
    em.on("sync", loads)

    cp = P.sem("cast_p")

    def pool(g):
        g.iota(t["iid"][:], [[1, 128]], base=0, channel_multiplier=-1,
               allow_small_or_imprecise_dtypes=True).then_inc(cp.h, 1)
    iota_v = cp.inc()
    em.on("gpsimd", pool)

    nld = ld.n

    def vec(v, nld=nld, iv=iota_v):
        v.memset(t["ones_row"][:], 1.0)
        v.memset(t["ones_col"][:], 1.0)
        v.memset(t["eps0"][:], EPS0)
        v.memset(t["eps1"][:], EPS1)
        v.wait_ge(cp.h, iv)
        v.tensor_scalar(out=t["ident"][:], in0=t["iid"][:], scalar1=0.0, scalar2=None,
                        op0=OP.is_equal)
        v.wait_ge(ld.h, nld)
        v.tensor_copy(out=t["bv_rb"][:], in_=t["bv_row"][:])
        v.tensor_scalar(out=t["slope_nb"][:], in0=t["slope_b"][:], scalar1=-1.0,
                        scalar2=None, op0=OP.mult)
    em.on("vector", vec)
    em.flush()
    P.barrier()


def ph_ln(P, x_name, xn_name):
    nc = P.nc
    pfx = "ln" + x_name[:2]
    es = ExitStack()
    xf = [es.enter_context(nc.sbuf_tensor(f"{pfx}_xf{i}", [128, NDC * TB], F32)) for i in range(2)]
    xb = es.enter_context(nc.sbuf_tensor(f"{pfx}_xb", [128, NDC * TB], BF16))
    sq = es.enter_context(nc.sbuf_tensor(f"{pfx}_sq", [128, NDC * TB], BF16))
    xn_t = [es.enter_context(nc.sbuf_tensor(f"{pfx}_xn{i}", [128, NDC * TB], BF16)) for i in range(2)]
    tmp = es.enter_context(nc.sbuf_tensor(f"{pfx}_tmp", [128, TB], F32))
    nm_bf = es.enter_context(nc.sbuf_tensor(f"{pfx}_nm", [1, TB], BF16))
    mf = es.enter_context(nc.sbuf_tensor(f"{pfx}_mf", [1, TB], F32))
    msq = es.enter_context(nc.sbuf_tensor(f"{pfx}_msq", [1, TB], F32))
    var = es.enter_context(nc.sbuf_tensor(f"{pfx}_var", [1, TB], F32))
    lnv = es.enter_context(nc.sbuf_tensor(f"{pfx}_lnv", [1, TB], F32))
    rstd_bf = es.enter_context(nc.sbuf_tensor(f"{pfx}_rs", [1, TB], BF16))
    nm_b = es.enter_context(nc.sbuf_tensor(f"{pfx}_nmb", [128, TB], F32))
    rs_b = es.enter_context(nc.sbuf_tensor(f"{pfx}_rsb", [128, TB], F32))
    pa = [es.enter_context(nc.psum_tensor(f"{pfx}_pa{i}", [128, TB], F32)) for i in range(2)]
    pb = [es.enter_context(nc.psum_tensor(f"{pfx}_pb{i}", [128, TB], F32)) for i in range(2)]
    pc = [es.enter_context(nc.psum_tensor(f"{pfx}_pc{i}", [128, TB], F32)) for i in range(2)]
    pd = [es.enter_context(nc.psum_tensor(f"{pfx}_pd{i}", [128, TB], F32)) for i in range(2)]

    ldx = [P.dsem("ld_a0"), P.dsem("ld_a1")]
    mm = P.sem("mm")
    evv = P.sem("ev_v")
    evs = P.sem("ev_s")
    stv = P.dsem("st_v")
    ones_row, ones_col = P.t["ones_row"], P.t["ones_col"]
    X, XN = P.dram[x_name], P.dram[xn_name]

    em = Em(P)
    apply_done = [0, 0]
    strip_done = [0, 0]
    bcast_evac = [0, 0]
    store_done = [0, 0]

    for it in range(NSEQ * (L // TB)):
        s, r = divmod(it, L // TB)
        sl = it % 2
        gate = apply_done[sl]

        def load(q, s=s, r=r, sl=sl, gate=gate):
            if gate:
                q.wait_ge(evv.h, gate)
            q.dma_start(out=_t3(xf[sl], NDC, TB), in_=_fm_in(X[s], r * TB, (r + 1) * TB)
                        ).then_inc(ldx[sl].h, 16)
        ldv = ldx[sl].inc(16)
        em.on("sync", load)

        def cast(v, sl=sl, ldv=ldv):
            v.wait_ge(ldx[sl].h, ldv)
            v.tensor_copy(out=xb[:], in_=xf[sl][:]).then_inc(evv.h, 1)
        cast_v = evv.inc()
        em.on("vector", cast)

        def square(sc, cv=cast_v):
            sc.wait_ge(evv.h, cv)
            sc.activation(out=sq[:], in_=xb[:], func=AF.Square).then_inc(evs.h, 1)
        sq_v = evs.inc()
        em.on("scalar", square)

        def stats(te, sl=sl, cv=cast_v, sv=sq_v, sd=strip_done[sl]):
            te.wait_ge(evv.h, cv)
            te.wait_ge(evs.h, sv)
            if sd:
                te.wait_ge(evv.h, sd)
            for c in range(NDC):
                te.matmul(pa[sl][0:1, :], ones_col[:], xb[:, c * TB:(c + 1) * TB],
                          start=(c == 0), stop=(c == NDC - 1))
            for c in range(NDC):
                ins = te.matmul(pb[sl][0:1, :], ones_col[:], sq[:, c * TB:(c + 1) * TB],
                                start=(c == 0), stop=(c == NDC - 1))
            ins.then_inc(mm.h, 1)
        mm_v = mm.inc()
        em.on("tensor", stats)

        def strips(v, sl=sl, mv=mm_v):
            v.wait_ge(mm.h, mv)
            v.tensor_scalar(out=nm_bf[:], in0=pa[sl][0:1, :], scalar1=-1.0 / D,
                            scalar2=None, op0=OP.mult)
            v.tensor_scalar(out=mf[:], in0=pa[sl][0:1, :], scalar1=1.0 / D,
                            scalar2=None, op0=OP.mult)
            v.tensor_tensor(out=msq[:], in0=mf[:], in1=mf[:], op=OP.mult)
            v.tensor_scalar(out=var[:], in0=pb[sl][0:1, :], scalar1=1.0 / D,
                            scalar2=None, op0=OP.mult)
            v.tensor_tensor(out=var[:], in0=var[:], in1=msq[:], op=OP.subtract
                            ).then_inc(evv.h, 1)
        strip_v = evv.inc()
        strip_done[sl] = strip_v
        em.on("vector", strips)

        def rstd(sc, sv=strip_v):
            sc.wait_ge(evv.h, sv)
            sc.activation(out=lnv[:], in_=var[:], func=AF.Ln, bias=P.t['eps0'][0:1, 0:1])
            sc.activation(out=rstd_bf[:], in_=lnv[:], func=AF.Exp, scale=-0.5
                          ).then_inc(evs.h, 1)
        rstd_v = evs.inc()
        em.on("scalar", rstd)

        def bcast(te, sl=sl, rv=rstd_v, sv=strip_v, be=bcast_evac[sl]):
            te.wait_ge(evs.h, rv)
            if be:
                te.wait_ge(evv.h, be)
            te.matmul(pc[sl][:], ones_row[:], nm_bf[:], start=True, stop=True)
            te.matmul(pd[sl][:], ones_row[:], rstd_bf[:], start=True, stop=True
                      ).then_inc(mm.h, 1)
        mm_v2 = mm.inc()
        em.on("tensor", bcast)

        def apply(v, sl=sl, mv=mm_v2, sg=store_done[sl]):
            v.wait_ge(mm.h, mv)
            if sg:
                v.wait_ge(stv.h, sg)
            v.tensor_copy(out=nm_b[:], in_=pc[sl][:])
            v.tensor_copy(out=rs_b[:], in_=pd[sl][:])
            for c in range(NDC):
                v.tensor_tensor(out=tmp[:], in0=xf[sl][:, c * TB:(c + 1) * TB],
                                in1=nm_b[:], op=OP.add)
                ins = v.tensor_tensor(out=xn_t[sl][:, c * TB:(c + 1) * TB], in0=tmp[:],
                                      in1=rs_b[:], op=OP.mult)
            ins.then_inc(evv.h, 1)
        ap_v = evv.inc()
        apply_done[sl] = ap_v
        bcast_evac[sl] = ap_v
        em.on("vector", apply)

        def store(g, s=s, r=r, sl=sl, av=ap_v):
            g.wait_ge(evv.h, av)
            g.dma_start(out=_fm_in(XN[s], r * TB, (r + 1) * TB), in_=_t3(xn_t[sl], NDC, TB)
                        ).then_inc(stv.h, 16)
        store_done[sl] = stv.inc(16)
        em.on("gpsimd", store)

    em.flush()
    es.close()
    P.barrier()


def ph_qkv0(P):
    nc = P.nc
    es = ExitStack()
    xnb = es.enter_context(nc.sbuf_tensor("q0_xnb", [128, NDC * L], BF16))
    Bbf = {nm: es.enter_context(nc.sbuf_tensor(f"q0_{nm}b", [128, NDC * 16], BF16))
           for nm in ("Bq", "Bk", "Bv")}
    Abf = {nm: es.enter_context(nc.sbuf_tensor(f"q0_{nm}b", [16, D], BF16))
           for nm in ("Aq", "Ak", "Av")}
    tmpb = {nm: es.enter_context(nc.sbuf_tensor(f"q0_t{nm}", [16, L], BF16))
            for nm in ("q", "k", "v")}
    wst = [es.enter_context(nc.sbuf_tensor(f"q0_wst{i}", [128, NDC * 128], F32)) for i in range(2)]
    wbf = [es.enter_context(nc.sbuf_tensor(f"q0_wbf{i}", [128, NDC * 128], BF16)) for i in range(2)]
    vst = es.enter_context(nc.sbuf_tensor("q0_vst", [128, NDC * TB], F32))
    vbf = [es.enter_context(nc.sbuf_tensor(f"q0_vbf{i}", [128, NDC * TB], BF16)) for i in range(2)]
    otile = [es.enter_context(nc.sbuf_tensor(f"q0_ot{i}", [128, TB], BF16)) for i in range(4)]
    vtile = [es.enter_context(nc.sbuf_tensor(f"q0_vt{i}", [128, TB], BF16)) for i in range(4)]
    ttile = [es.enter_context(nc.sbuf_tensor(f"q0_tt{i}", [16, TB], BF16)) for i in range(2)]
    pb = [es.enter_context(nc.psum_tensor(f"q0_p{i}", [128, TB], F32)) for i in range(8)]

    ldx = P.dsem("ld_b")
    ldw = [P.dsem("ld_w0"), P.dsem("ld_w1")]
    ldc = P.dsem("ld_c")
    cp = P.sem("cast_p")
    mm = P.sem("mm")
    evv = P.sem("ev_v")
    evs = P.sem("ev_s")
    sts = P.dsem("st_s")
    stg = P.dsem("st_g")
    t = P.t
    em = Em(P)

    bank_free = [("mm", 0)] * 8
    sem_of = {"mm": mm, "ev_v": evv, "ev_s": evs}

    def acquire(te, b):
        nm, v = bank_free[b]
        if v:
            te.wait_ge(sem_of[nm].h, v)

    wslot_cast = [0, 0]
    wbf_free = [0, 0]
    vbf_free = [0, 0]
    vst_cast = [0]
    ot_free = [0] * 4
    vt_free = [0] * 4
    xn_free = 0
    ldv = P.dsem("ld_d")

    for s in range(NSEQ):
        def ldseq(q, s=s, gate=xn_free):
            if gate:
                q.wait_ge(mm.h, gate)
            q.dma_start(out=_t3(xnb, NDC, L), in_=_fm_in(P.dram["XN"][s])).then_inc(ldx.h, 16)
            if s == 0:
                for i, nm in enumerate(("Bq", "Bk", "Bv")):
                    dst = vst[:, D + i * 256: D + (i + 1) * 256]
                    q.dma_start(out=dst.rearrange("p (c w) -> p c w", c=NDC),
                                in_=_fm_in(P.dram[nm + "_t"])).then_inc(ldc.h, 16)
                astage = [wst[0][0:16, :], wst[1][0:16, :], vst[0:16, 0:D]]
                for i, nm in enumerate(("Aq", "Ak", "Av")):
                    q.dma_start(out=astage[i], in_=P.dram[nm + "_t"]).then_inc(ldc.h, 16)
        ldx_v = ldx.inc(16)
        if s == 0:
            ldc_v = ldc.inc(16 * 6)
        em.on("sync", ldseq)

        if s == 0:
            def castc(g, lv=ldc_v):
                g.wait_ge(ldc.h, lv)
                for i, nm in enumerate(("Bq", "Bk", "Bv")):
                    g.tensor_copy(out=Bbf[nm][:], in_=vst[:, D + i * 256: D + (i + 1) * 256])
                astage = [wst[0][0:16, :], wst[1][0:16, :], vst[0:16, 0:D]]
                for i, nm in enumerate(("Aq", "Ak", "Av")):
                    ins = g.tensor_copy(out=Abf[nm][:], in_=astage[i])
                ins.then_inc(cp.h, 1)
            cast_c = cp.inc()
            em.on("gpsimd", castc)
            wslot_cast[0] = wslot_cast[1] = cast_c
            vst_cast[0] = cast_c

        for i, nm in enumerate(("q", "k", "v")):
            Bb = Bbf["B" + nm]
            for tb in range(4):
                b = (i * 4 + tb) % 2 + 6
                def tmm(te, b=b, Bb=Bb, tb=tb, lv=ldx_v, cc=cast_c, fr=bank_free[b]):
                    te.wait_ge(ldx.h, lv)
                    te.wait_ge(cp.h, cc)
                    nm_, v_ = fr
                    if v_:
                        te.wait_ge(sem_of[nm_].h, v_)
                    for c in range(NDC):
                        ins = te.matmul(pb[b][0:16, :], Bb[:, c * 16:(c + 1) * 16],
                                        xnb[:, c * L + tb * TB: c * L + (tb + 1) * TB],
                                        start=(c == 0), stop=(c == NDC - 1))
                    ins.then_inc(mm.h, 1)
                mm_v = mm.inc()
                em.on("tensor", tmm)

                def tevac(sc, b=b, nm=nm, tb=tb, mv=mm_v):
                    sc.wait_ge(mm.h, mv)
                    sc.activation(out=tmpb[nm][:, tb * TB:(tb + 1) * TB], in_=pb[b][0:16, :],
                                  func=AF.Copy, scale=0.5).then_inc(evs.h, 1)
                ev_val = evs.inc()
                bank_free[b] = ("ev_s", ev_val)
                em.on("scalar", tevac)
        tmp_done = evs.n

        for wi, (wnm, onm, bnm, anm) in enumerate(
                (("wq_t", "QF", "bq_c", "Aq"), ("wk_t", "KF", "bk_c", "Ak"))):
            Ab = Abf[anm]
            for oc in range(NDC):
                it = (s * 2 + wi) * NDC + oc
                sl = it % 2

                def wload(q, wnm=wnm, oc=oc, sl=sl, gate=wslot_cast[sl]):
                    if gate:
                        q.wait_ge(cp.h, gate)
                    q.dma_start(out=_t3(wst[sl], NDC, 128),
                                in_=_fm_in(P.dram[wnm], oc * 128, (oc + 1) * 128)
                                ).then_inc(ldw[sl].h, 16)
                ldw_v = ldw[sl].inc(16)
                em.on("sync", wload)

                def wcast(g, sl=sl, lv=ldw_v, gate=wbf_free[sl]):
                    g.wait_ge(ldw[sl].h, lv)
                    if gate:
                        g.wait_ge(mm.h, gate)
                    g.tensor_copy(out=wbf[sl][:], in_=wst[sl][:]).then_inc(cp.h, 1)
                wslot_cast[sl] = cp.inc()
                cast_v = cp.n
                em.on("gpsimd", wcast)

                for tb in range(4):
                    b = tb % 3
                    def qmm(te, b=b, sl=sl, tb=tb, oc=oc, Ab=Ab, cv=cast_v,
                            td=tmp_done, fr=bank_free[b], nm=wnm[1]):
                        te.wait_ge(cp.h, cv)
                        if tb == 0:
                            te.wait_ge(evs.h, td)
                        nm_, v_ = fr
                        if v_:
                            te.wait_ge(sem_of[nm_].h, v_)
                        for c in range(NDC):
                            te.matmul(pb[b][:], wbf[sl][:, c * 128:(c + 1) * 128],
                                      xnb[:, c * L + tb * TB: c * L + (tb + 1) * TB],
                                      start=(c == 0), stop=False)
                        te.matmul(pb[b][:], Ab[0:16, oc * 128:(oc + 1) * 128],
                                  tmpb[nm][0:16, tb * TB:(tb + 1) * TB],
                                  start=False, stop=True).then_inc(mm.h, 1)
                    mm_v = mm.inc()
                    em.on("tensor", qmm)

                    osl = tb
                    def qevac(v, b=b, osl=osl, bnm=bnm, oc=oc, mv=mm_v, fr=ot_free[osl]):
                        v.wait_ge(mm.h, mv)
                        if fr:
                            v.wait_ge(sts.h, fr)
                        v.tensor_scalar(out=otile[osl][:], in0=pb[b][:],
                                        scalar1=t[bnm][:, oc:oc + 1], scalar2=None,
                                        op0=OP.add).then_inc(evv.h, 1)
                    ev_val = evv.inc()
                    bank_free[b] = ("ev_v", ev_val)
                    em.on("vector", qevac)

                    def qstore(g, osl=osl, onm=onm, s=s, oc=oc, tb=tb, ev=ev_val):
                        g.wait_ge(evv.h, ev)
                        g.dma_start(out=P.dram[onm][s, oc * 128:(oc + 1) * 128,
                                                    tb * TB:(tb + 1) * TB],
                                    in_=otile[osl][:]).then_inc(sts.h, 16)
                    ot_free[osl] = sts.inc(16)
                    em.on("gpsimd", qstore)
                wbf_free[sl] = mm.n

        for eb in range(4):
            vsl = eb % 2

            def vload(q, eb=eb, gate=vst_cast[0]):
                if gate:
                    q.wait_ge(cp.h, gate)
                q.dma_start(out=_t3(vst, NDC, TB),
                            in_=_fm_in(P.dram["wv_t"], eb * TB, (eb + 1) * TB)
                            ).then_inc(ldv.h, 16)
            ldv_v = ldv.inc(16)
            em.on("sync", vload)

            def vcast(g, lv=ldv_v, vsl=vsl, gate=vbf_free[vsl]):
                g.wait_ge(ldv.h, lv)
                if gate:
                    g.wait_ge(mm.h, gate)
                g.tensor_copy(out=vbf[vsl][:], in_=vst[:]).then_inc(cp.h, 1)
            cast_v = cp.inc()
            vst_cast[0] = cast_v
            em.on("gpsimd", vcast)

            for tc in range(NDC):
                b = 3 + tc % 3
                def vmm(te, b=b, vsl=vsl, tc=tc, eb=eb, cv=cast_v, td=tmp_done,
                        fr=bank_free[b]):
                    te.wait_ge(cp.h, cv)
                    if tc == 0:
                        te.wait_ge(evs.h, td)
                    nm_, v_ = fr
                    if v_:
                        te.wait_ge(sem_of[nm_].h, v_)
                    for c in range(NDC):
                        te.matmul(pb[b][:], xnb[:, c * L + tc * 128: c * L + tc * 128 + 128],
                                  vbf[vsl][:, c * TB:(c + 1) * TB],
                                  start=(c == 0), stop=False)
                    te.matmul(pb[b][:], P.t["ones_row"][:],
                              t["bv_rb"][0:1, eb * TB:(eb + 1) * TB], start=False, stop=False)
                    te.matmul(pb[b][:], tmpb["v"][0:16, tc * 128:tc * 128 + 128],
                              Abf["Av"][0:16, eb * TB:(eb + 1) * TB],
                              start=False, stop=True).then_inc(mm.h, 1)
                mm_v = mm.inc()
                em.on("tensor", vmm)

                vtl = tc % 4
                def vevac(sc, b=b, vtl=vtl, mv=mm_v, fr=vt_free[vtl]):
                    sc.wait_ge(mm.h, mv)
                    if fr:
                        sc.wait_ge(stg.h, fr)
                    sc.activation(out=vtile[vtl][:], in_=pb[b][:], func=AF.Copy
                                  ).then_inc(evs.h, 1)
                ev_val = evs.inc()
                bank_free[b] = ("ev_s", ev_val)
                em.on("scalar", vevac)

                def vstore(sc, vtl=vtl, s=s, tc=tc, eb=eb, ev=ev_val):
                    sc.wait_ge(evs.h, ev)
                    sc.dma_start(out=P.dram["VT"][s, tc * 128:(tc + 1) * 128,
                                                  eb * TB:(eb + 1) * TB],
                                 in_=vtile[vtl][:]).then_inc(stg.h, 16)
                vt_free[vtl] = stg.inc(16)
                em.on("scalar", vstore)
            vbf_free[vsl] = mm.n

        xn_free = mm.n

    em.flush()
    es.close()
    P.barrier()


def _fsum_chain_ops(P, es):
    nc = P.nc
    facc = [es.enter_context(nc.sbuf_tensor(f"fs_acc{i}", [128, D], F32)) for i in range(2)]
    fbf = [es.enter_context(nc.sbuf_tensor(f"fs_bf{i}", [128, D], BF16)) for i in range(2)]
    ldf = [P.dsem("ld_f0"), P.dsem("ld_f1")]
    stf = P.dsem("st_f")
    cpf = P.sem("cast_f")
    chain_free = [0, 0]
    ops = []
    for c in range(NDC):
        par = c % 2
        sem = ldf[par]

        def first(g, c=c, par=par, gate=chain_free[par], sem=sem):
            if gate:
                g.wait_ge(stf.h, gate)
            g.dma_start(out=facc[par][:], in_=P.dram["factors"][0, c * 128:(c + 1) * 128, :]
                        ).then_inc(sem.h, 16)
        sem.inc(16)
        ops.append(first)
        for r in range(1, 8):
            def acc(g, c=c, par=par, r=r, lv=sem.n, sem=sem):
                g.wait_ge(sem.h, lv)
                g.dma_start(out=facc[par][:], in_=P.dram["factors"][r, c * 128:(c + 1) * 128, :],
                            accum_op=OP.add).then_inc(sem.h, 16)
            sem.inc(16)
            ops.append(acc)

        def cast(g, par=par, lv=sem.n, sem=sem):
            g.wait_ge(sem.h, lv)
            g.tensor_copy(out=fbf[par][:], in_=facc[par][:]).then_inc(cpf.h, 1)
        cast_v = cpf.inc()
        ops.append(cast)

        def store(g, c=c, par=par, cv=cast_v):
            g.wait_ge(cpf.h, cv)
            g.dma_start(out=P.dram["FSUM"][c * 128:(c + 1) * 128, :], in_=fbf[par][:]
                        ).then_inc(stf.h, 16)
        chain_free[par] = stf.inc(16)
        ops.append(store)
    return ops


def ph_att0(P):
    nc = P.nc
    es = ExitStack()
    qh = [es.enter_context(nc.sbuf_tensor(f"a0_qh{i}", [128, L], BF16)) for i in range(2)]
    kh = [es.enter_context(nc.sbuf_tensor(f"a0_kh{i}", [128, L], BF16)) for i in range(2)]
    vh = [es.enter_context(nc.sbuf_tensor(f"a0_vh{i}", [128, L], BF16)) for i in range(2)]
    esb = es.enter_context(nc.sbuf_tensor("a0_es", [128, NDC * TB], BF16))
    rrow = es.enter_context(nc.sbuf_tensor("a0_rr", [1, TB], F32))
    rrow_bf = es.enter_context(nc.sbuf_tensor("a0_rrb", [1, TB], BF16))
    rb_sb = [es.enter_context(nc.sbuf_tensor(f"a0_rb{i}", [128, TB], F32)) for i in range(2)]
    ao_t = [es.enter_context(nc.sbuf_tensor(f"a0_ao{i}", [128, TB], BF16)) for i in range(2)]
    ps = [es.enter_context(nc.psum_tensor(f"a0_ps{i}", [128, TB], F32)) for i in range(2)]
    po = [es.enter_context(nc.psum_tensor(f"a0_po{i}", [128, TB], F32)) for i in range(2)]
    psm = [es.enter_context(nc.psum_tensor(f"a0_pm{i}", [128, TB], F32)) for i in range(2)]
    pbc = [es.enter_context(nc.psum_tensor(f"a0_pb{i}", [128, TB], F32)) for i in range(2)]

    ldq = [P.dsem("ld_w0"), P.dsem("ld_w1")]
    mm = P.sem("mm")
    evv = P.sem("ev_v")
    evs = P.sem("ev_s")
    stg = P.dsem("st_g")
    t = P.t
    em = Em(P)
    fsum_ops = _fsum_chain_ops(P, es) if not getattr(ph_att0, "skip_fsum", False) else []
    fsum_i = 0

    it_mm_done = [0, 0]
    nit = getattr(ph_att0, 'nit', NSEQ * HEADS)
    ao_free = [0, 0]
    rb_free = [0, 0]
    ps_free = [("ev_s", 0)] * 2
    po_free = [("ev_v", 0)] * 2
    pm_free = [("ev_v", 0)] * 2
    pbc_free = [("ev_v", 0)] * 2
    sem_of = {"mm": mm, "ev_v": evv, "ev_s": evs}

    for it in range(nit):
        s, h = divmod(it, HEADS)
        sl = it % 2

        def load(q, s=s, h=h, sl=sl, gate=it_mm_done[sl]):
            if gate:
                q.wait_ge(mm.h, gate)
            q.dma_start(out=qh[sl][:], in_=P.dram["QF"][s, h * 128:(h + 1) * 128, :]
                        ).then_inc(ldq[sl].h, 16)
            q.dma_start(out=kh[sl][:], in_=P.dram["KF"][s, h * 128:(h + 1) * 128, :]
                        ).then_inc(ldq[sl].h, 16)
            q.dma_start(out=_t3(vh[sl], NDC, 128),
                        in_=P.dram["VT"][s][:, h * 128:(h + 1) * 128]
                        .rearrange("(c p) e -> p c e", p=128)).then_inc(ldq[sl].h, 16)
        ld_v = ldq[sl].inc(48)
        em.on("sync", load)

        for qb in range(4):
            qsl = qb % 2
            exp_vals = []
            for kc in range(NDC):
                b = kc % 2

                def smm(te, sl=sl, kc=kc, qb=qb, b=b, lv=ld_v, fr=ps_free[b]):
                    if kc == 0 and qb == 0:
                        te.wait_ge(ldq[sl].h, lv)
                    nm_, v_ = fr
                    if v_:
                        te.wait_ge(sem_of[nm_].h, v_)
                    te.matmul(ps[b][:], kh[sl][:, kc * 128:(kc + 1) * 128],
                              qh[sl][:, qb * TB:(qb + 1) * TB],
                              start=True, stop=True).then_inc(mm.h, 1)
                mm_v = mm.inc()
                em.on("tensor", smm)

                def expev(sc, b=b, kc=kc, mv=mm_v):
                    sc.wait_ge(mm.h, mv)
                    sc.activation(out=esb[:, kc * TB:(kc + 1) * TB], in_=ps[b][:],
                                  func=AF.Exp, scale=SCALE0).then_inc(evs.h, 1)
                ev_val = evs.inc()
                ps_free[b] = ("ev_s", ev_val)
                exp_vals.append(ev_val)
                em.on("scalar", expev)

            mode = getattr(ph_att0, "mode", 3)
            if mode == 1:
                def fake(v, qsl=qsl, evs_vals=tuple(exp_vals), gate=ao_free[qsl]):
                    v.wait_ge(evs.h, evs_vals[-1])
                    if gate:
                        v.wait_ge(stg.h, gate)
                    v.tensor_copy(out=ao_t[qsl][:], in_=esb[:, 0:TB]).then_inc(evv.h, 1)
                ev_val = evv.inc()
                em.on("vector", fake)

                def store1(g, s=s, h=h, qb=qb, qsl=qsl, ev=ev_val):
                    g.wait_ge(evv.h, ev)
                    g.dma_start(out=P.dram["AO"][s, h * 128:(h + 1) * 128, qb * TB:(qb + 1) * TB],
                                in_=ao_t[qsl][:]).then_inc(stg.h, 16)
                ao_free[qsl] = stg.inc(16)
                em.on("gpsimd", store1)
                continue

            def vmms(te, sl=sl, qsl=qsl, evs_vals=tuple(exp_vals),
                     fro=po_free[qsl], frm=pm_free[qsl]):
                for nm_, v_ in (fro, frm):
                    if v_:
                        te.wait_ge(sem_of[nm_].h, v_)
                for kc in range(NDC):
                    te.wait_ge(evs.h, evs_vals[kc])
                    te.matmul(po[qsl][:], vh[sl][:, kc * 128:(kc + 1) * 128],
                              esb[:, kc * TB:(kc + 1) * TB],
                              start=(kc == 0), stop=(kc == NDC - 1))
                for kc in range(NDC):
                    ins = te.matmul(psm[qsl][0:1, :], t["ones_col"][:],
                                    esb[:, kc * TB:(kc + 1) * TB],
                                    start=(kc == 0), stop=(kc == NDC - 1))
                ins.then_inc(mm.h, 1)
            mm_v2 = mm.inc()
            em.on("tensor", vmms)

            if mode == 2:
                def evac2(v, qsl=qsl, mv=mm_v2, gate=ao_free[qsl]):
                    v.wait_ge(mm.h, mv)
                    if gate:
                        v.wait_ge(stg.h, gate)
                    v.tensor_copy(out=ao_t[qsl][:], in_=po[qsl][:]).then_inc(evv.h, 1)
                ev_val = evv.inc()
                po_free[qsl] = ("ev_v", ev_val)
                pm_free[qsl] = ("ev_v", ev_val)
                em.on("vector", evac2)

                def store2(g, s=s, h=h, qb=qb, qsl=qsl, ev=ev_val):
                    g.wait_ge(evv.h, ev)
                    g.dma_start(out=P.dram["AO"][s, h * 128:(h + 1) * 128, qb * TB:(qb + 1) * TB],
                                in_=ao_t[qsl][:]).then_inc(stg.h, 16)
                ao_free[qsl] = stg.inc(16)
                em.on("gpsimd", store2)
                continue

            def recip(v, qsl=qsl, mv=mm_v2):
                v.wait_ge(mm.h, mv)
                v.tensor_copy(out=rrow[:], in_=psm[qsl][0:1, :])
                v.reciprocal(out=rrow[:], in_=rrow[:])
                v.tensor_copy(out=rrow_bf[:], in_=rrow[:]).then_inc(evv.h, 1)
            rec_v = evv.inc()
            pm_free[qsl] = ("ev_v", rec_v)
            em.on("vector", recip)

            def bc(te, qsl=qsl, rv=rec_v, fr=pbc_free[qsl]):
                te.wait_ge(evv.h, rv)
                nm_, v_ = fr
                if v_:
                    te.wait_ge(sem_of[nm_].h, v_)
                te.matmul(pbc[qsl][:], t["ones_row"][:], rrow_bf[:],
                          start=True, stop=True).then_inc(mm.h, 1)
            mm_v3 = mm.inc()
            em.on("tensor", bc)

            def norm(v, qsl=qsl, mv=mm_v3, gate=ao_free[qsl]):
                v.wait_ge(mm.h, mv)
                if gate:
                    v.wait_ge(stg.h, gate)
                v.tensor_copy(out=rb_sb[qsl][:], in_=pbc[qsl][:])
                if getattr(ph_att0, "mode", 3) == 4:
                    v.tensor_copy(out=ao_t[qsl][:], in_=po[qsl][:]).then_inc(evv.h, 1)
                else:
                    v.tensor_tensor(out=ao_t[qsl][:], in0=po[qsl][:], in1=rb_sb[qsl][:],
                                    op=OP.mult).then_inc(evv.h, 1)
            ev_val = evv.inc()
            pbc_free[qsl] = ("ev_v", ev_val)
            po_free[qsl] = ("ev_v", ev_val)
            em.on("vector", norm)

            def store(g, s=s, h=h, qb=qb, qsl=qsl, ev=ev_val):
                g.wait_ge(evv.h, ev)
                g.dma_start(out=P.dram["AO"][s, h * 128:(h + 1) * 128, qb * TB:(qb + 1) * TB],
                            in_=ao_t[qsl][:]).then_inc(stg.h, 16)
            ao_free[qsl] = stg.inc(16)
            em.on("gpsimd", store)

        it_mm_done[sl] = mm.n
        take = max(1, (len(fsum_ops) - fsum_i + (nit - it) - 1) // max(1, nit - it))
        for _ in range(take):
            if fsum_i < len(fsum_ops):
                em.on("gpsimd", fsum_ops[fsum_i])
                fsum_i += 1
    while fsum_i < len(fsum_ops):
        em.on("gpsimd", fsum_ops[fsum_i])
        fsum_i += 1

    em.flush()
    es.close()
    P.barrier()


def ph_r0a(P):
    nc = P.nc
    es = ExitStack()
    aob = es.enter_context(nc.sbuf_tensor("ra_ao", [128, NDC * L], BF16))
    wst = [es.enter_context(nc.sbuf_tensor(f"ra_wst{i}", [128, NDC * 128], F32)) for i in range(2)]
    wbf = [es.enter_context(nc.sbuf_tensor(f"ra_wbf{i}", [128, NDC * 128], BF16)) for i in range(2)]
    Bbf = es.enter_context(nc.sbuf_tensor("ra_Bb", [128, NDC * 16], BF16))
    Abf = es.enter_context(nc.sbuf_tensor("ra_Ab", [16, D], BF16))
    tmpo = es.enter_context(nc.sbuf_tensor("ra_tmp", [16, L], BF16))
    otile = [es.enter_context(nc.sbuf_tensor(f"ra_ot{i}", [128, TB], F32)) for i in range(4)]
    pb = [es.enter_context(nc.psum_tensor(f"ra_p{i}", [128, TB], F32)) for i in range(6)]

    ldx = P.dsem("ld_b")
    ldw = [P.dsem("ld_w0"), P.dsem("ld_w1")]
    ldc = P.dsem("ld_c")
    cp = P.sem("cast_p")
    mm = P.sem("mm")
    evv = P.sem("ev_v")
    sts = P.dsem("st_s")
    t = P.t
    em = Em(P)
    sem_of = {"mm": mm, "ev_v": evv}
    bank_free = [("mm", 0)] * 6
    wslot_cast = [0, 0]
    wbf_free = [0, 0]
    ot_free = [0] * 4
    ao_free = 0

    for s in range(NSEQ):
        def ldseq(q, s=s, gate=ao_free):
            if gate:
                q.wait_ge(mm.h, gate)
            q.dma_start(out=_t3(aob, NDC, L), in_=_fm_in(P.dram["AO"][s])).then_inc(ldx.h, 16)
            if s == 0:
                q.dma_start(out=wst[0][:, 0:NDC * 16].rearrange("p (c w) -> p c w", c=NDC),
                            in_=_fm_in(P.dram["Bo_t"])).then_inc(ldc.h, 16)
                q.dma_start(out=wst[1][0:16, :], in_=P.dram["Ao_t"]).then_inc(ldc.h, 16)
        ldx_v = ldx.inc(16)
        if s == 0:
            ldc_v = ldc.inc(32)
        em.on("sync", ldseq)

        if s == 0:
            def castc(g, lv=ldc_v):
                g.wait_ge(ldc.h, lv)
                g.tensor_copy(out=Bbf[:], in_=wst[0][:, 0:NDC * 16])
                g.tensor_copy(out=Abf[:], in_=wst[1][0:16, :]).then_inc(cp.h, 1)
            cast_c = cp.inc()
            wslot_cast = [cast_c, cast_c]
            em.on("gpsimd", castc)

        for tb in range(4):
            b = 4 + tb % 2
            def tmm(te, b=b, tb=tb, lv=ldx_v, cc=cast_c, fr=bank_free[b]):
                te.wait_ge(ldx.h, lv)
                te.wait_ge(cp.h, cc)
                nm_, v_ = fr
                if v_:
                    te.wait_ge(sem_of[nm_].h, v_)
                for c in range(NDC):
                    ins = te.matmul(pb[b][0:16, :], Bbf[:, c * 16:(c + 1) * 16],
                                    aob[:, c * L + tb * TB: c * L + (tb + 1) * TB],
                                    start=(c == 0), stop=(c == NDC - 1))
                ins.then_inc(mm.h, 1)
            mm_v = mm.inc()
            em.on("tensor", tmm)

            def tevac(v, b=b, tb=tb, mv=mm_v):
                v.wait_ge(mm.h, mv)
                v.tensor_scalar(out=tmpo[:, tb * TB:(tb + 1) * TB], in0=pb[b][0:16, :],
                                scalar1=0.5, scalar2=None, op0=OP.mult).then_inc(evv.h, 1)
            ev_val = evv.inc()
            bank_free[b] = ("ev_v", ev_val)
            em.on("vector", tevac)
        tmp_done = evv.n

        for oc in range(NDC):
            sl = (s * NDC + oc) % 2

            def wload(q, oc=oc, sl=sl, gate=wslot_cast[sl]):
                if gate:
                    q.wait_ge(cp.h, gate)
                q.dma_start(out=_t3(wst[sl], NDC, 128),
                            in_=_fm_in(P.dram["wo_t"], oc * 128, (oc + 1) * 128)
                            ).then_inc(ldw[sl].h, 16)
            ldw_v = ldw[sl].inc(16)
            em.on("sync", wload)

            def wcast(g, sl=sl, lv=ldw_v, gate=wbf_free[sl]):
                g.wait_ge(ldw[sl].h, lv)
                if gate:
                    g.wait_ge(mm.h, gate)
                g.tensor_copy(out=wbf[sl][:], in_=wst[sl][:]).then_inc(cp.h, 1)
            wslot_cast[sl] = cp.inc()
            cast_v = cp.n
            em.on("gpsimd", wcast)

            for tb in range(4):
                b = tb % 4
                def omm(te, b=b, sl=sl, tb=tb, oc=oc, cv=cast_v, td=tmp_done,
                        fr=bank_free[b]):
                    te.wait_ge(cp.h, cv)
                    if tb == 0:
                        te.wait_ge(evv.h, td)
                    nm_, v_ = fr
                    if v_:
                        te.wait_ge(sem_of[nm_].h, v_)
                    for c in range(NDC):
                        te.matmul(pb[b][:], wbf[sl][:, c * 128:(c + 1) * 128],
                                  aob[:, c * L + tb * TB: c * L + (tb + 1) * TB],
                                  start=(c == 0), stop=False)
                    te.matmul(pb[b][:], Abf[0:16, oc * 128:(oc + 1) * 128],
                              tmpo[0:16, tb * TB:(tb + 1) * TB],
                              start=False, stop=True).then_inc(mm.h, 1)
                mm_v = mm.inc()
                em.on("tensor", omm)

                osl = tb
                def oevac(v, b=b, osl=osl, oc=oc, mv=mm_v, fr=ot_free[osl]):
                    v.wait_ge(mm.h, mv)
                    if fr:
                        v.wait_ge(sts.h, fr)
                    v.tensor_scalar(out=otile[osl][:], in0=pb[b][:],
                                    scalar1=t["bo_c"][:, oc:oc + 1], scalar2=None,
                                    op0=OP.add).then_inc(evv.h, 1)
                ev_val = evv.inc()
                bank_free[b] = ("ev_v", ev_val)
                em.on("vector", oevac)

                def ostore(g, osl=osl, s=s, oc=oc, tb=tb, ev=ev_val):
                    g.wait_ge(evv.h, ev)
                    g.dma_start(out=P.dram["S0"][s, oc * 128:(oc + 1) * 128,
                                                 tb * TB:(tb + 1) * TB],
                                in_=otile[osl][:]).then_inc(sts.h, 16)
                ot_free[osl] = sts.inc(16)
                em.on("gpsimd", ostore)
            wbf_free[sl] = mm.n
        ao_free = mm.n

    em.flush()
    es.close()
    P.barrier()


def ph_r0b(P):
    nc = P.nc
    es = ExitStack()
    xnb = es.enter_context(nc.sbuf_tensor("rb_xn", [128, NDC * L], BF16))
    fst = [es.enter_context(nc.sbuf_tensor(f"rb_fs{i}", [128, NDC * 128], BF16)) for i in range(2)]
    g_sb = [es.enter_context(nc.sbuf_tensor(f"rb_g{i}", [128, TB], F32)) for i in range(2)]
    s0_t = [es.enter_context(nc.sbuf_tensor(f"rb_s0{i}", [128, TB], F32)) for i in range(2)]
    x_t = [es.enter_context(nc.sbuf_tensor(f"rb_x{i}", [128, TB], F32)) for i in range(2)]
    t1 = es.enter_context(nc.sbuf_tensor("rb_t1", [128, TB], F32))
    x1_t = [es.enter_context(nc.sbuf_tensor(f"rb_x1{i}", [128, TB], F32)) for i in range(2)]
    pb = [es.enter_context(nc.psum_tensor(f"rb_p{i}", [128, TB], F32)) for i in range(4)]

    ldx = P.dsem("ld_b")
    ldw = [P.dsem("ld_w0"), P.dsem("ld_w1")]
    lds2 = [P.dsem("ld_c"), P.dsem("ld_e")]
    mm = P.sem("mm")
    evv = P.sem("ev_v")
    evs = P.sem("ev_s")
    stg = P.dsem("st_g")
    t = P.t
    em = Em(P)
    sem_of = {"mm": mm, "ev_v": evv, "ev_s": evs}
    bank_free = [("mm", 0)] * 4
    fs_free = [0, 0]
    g_free = [0, 0]
    sx_free = [0, 0]
    x1_free = [0, 0]
    xn_free = 0

    for s in range(NSEQ):
        def ldseq(q, s=s, gate=xn_free):
            if gate:
                q.wait_ge(mm.h, gate)
            q.dma_start(out=_t3(xnb, NDC, L), in_=_fm_in(P.dram["XN"][s])).then_inc(ldx.h, 16)
        ldx_v = ldx.inc(16)
        em.on("sync", ldseq)

        for ec in range(NDC):
            wsl = (s * NDC + ec) % 2

            def fload(q, ec=ec, wsl=wsl, gate=fs_free[wsl]):
                if gate:
                    q.wait_ge(mm.h, gate)
                q.dma_start(out=_t3(fst[wsl], NDC, 128),
                            in_=_fm_in(P.dram["FSUM"], ec * 128, (ec + 1) * 128)
                            ).then_inc(ldw[wsl].h, 16)
            ldw_v = ldw[wsl].inc(16)
            em.on("sync", fload)

            for tb in range(4):
                b = tb % 4
                sl = tb % 2

                def fmm(te, b=b, wsl=wsl, tb=tb, lv=ldx_v, wv=ldw_v, fr=bank_free[b]):
                    te.wait_ge(ldx.h, lv)
                    te.wait_ge(ldw[wsl].h, wv)
                    nm_, v_ = fr
                    if v_:
                        te.wait_ge(sem_of[nm_].h, v_)
                    for c in range(NDC):
                        ins = te.matmul(pb[b][:], fst[wsl][:, c * 128:(c + 1) * 128],
                                        xnb[:, c * L + tb * TB: c * L + (tb + 1) * TB],
                                        start=(c == 0), stop=(c == NDC - 1))
                    ins.then_inc(mm.h, 1)
                mm_v = mm.inc()
                em.on("tensor", fmm)

                def gevac(sc, b=b, sl=sl, ec=ec, mv=mm_v, gate=g_free[sl]):
                    sc.wait_ge(mm.h, mv)
                    if gate:
                        sc.wait_ge(evv.h, gate)
                    sc.activation(out=g_sb[sl][:], in_=pb[b][:], func=AF.Silu,
                                  bias=t["fb_c"][:, ec:ec + 1], scale=RSQRT8
                                  ).then_inc(evs.h, 1)
                gev = evs.inc()
                bank_free[b] = ("ev_s", gev)
                em.on("scalar", gevac)

                def sxload(q, s=s, ec=ec, tb=tb, sl=sl, gate=sx_free[sl]):
                    if gate:
                        q.wait_ge(evv.h, gate)
                    q.dma_start(out=s0_t[sl][:],
                                in_=P.dram["S0"][s, ec * 128:(ec + 1) * 128,
                                                 tb * TB:(tb + 1) * TB]).then_inc(lds2[sl].h, 16)
                    q.dma_start(out=x_t[sl][:],
                                in_=P.dram["x"][s, ec * 128:(ec + 1) * 128,
                                                tb * TB:(tb + 1) * TB]).then_inc(lds2[sl].h, 16)
                lds_v = lds2[sl].inc(32)
                em.on("sync", sxload)

                def resid(v, sl=sl, gv=gev, lv=lds_v, gate=x1_free[sl]):
                    v.wait_ge(evs.h, gv)
                    v.wait_ge(lds2[sl].h, lv)
                    if gate:
                        v.wait_ge(stg.h, gate)
                    v.tensor_tensor(out=t1[:], in0=s0_t[sl][:], in1=g_sb[sl][:], op=OP.mult)
                    v.tensor_tensor(out=x1_t[sl][:], in0=t1[:], in1=x_t[sl][:],
                                    op=OP.add).then_inc(evv.h, 1)
                rv = evv.inc()
                g_free[sl] = rv
                sx_free[sl] = rv
                em.on("vector", resid)

                def x1store(g, s=s, ec=ec, tb=tb, sl=sl, ev=rv):
                    g.wait_ge(evv.h, ev)
                    g.dma_start(out=P.dram["X1"][s, ec * 128:(ec + 1) * 128,
                                                 tb * TB:(tb + 1) * TB],
                                in_=x1_t[sl][:]).then_inc(stg.h, 16)
                x1_free[sl] = stg.inc(16)
                em.on("gpsimd", x1store)
            fs_free[wsl] = mm.n
        xn_free = mm.n

    em.flush()
    es.close()
    P.barrier()


def ph_qkv1(P):
    nc = P.nc
    es = ExitStack()
    xnb = es.enter_context(nc.sbuf_tensor("q1_xnb", [128, NDC * L], BF16))
    wst = [es.enter_context(nc.sbuf_tensor(f"q1_wst{i}", [128, NDC * 128], F32)) for i in range(2)]
    wbf = [es.enter_context(nc.sbuf_tensor(f"q1_wbf{i}", [128, NDC * 128], BF16)) for i in range(2)]
    vst = es.enter_context(nc.sbuf_tensor("q1_vst", [128, NDC * TB], F32))
    vbf = [es.enter_context(nc.sbuf_tensor(f"q1_vbf{i}", [128, NDC * TB], BF16)) for i in range(2)]
    otile = [es.enter_context(nc.sbuf_tensor(f"q1_ot{i}", [128, TB], BF16)) for i in range(4)]
    pb = [es.enter_context(nc.psum_tensor(f"q1_p{i}", [128, TB], F32)) for i in range(6)]

    ldx = P.dsem("ld_b")
    ldw = [P.dsem("ld_w0"), P.dsem("ld_w1")]
    ldv = P.dsem("ld_d")
    cp = P.sem("cast_p")
    mm = P.sem("mm")
    evs = P.sem("ev_s")
    sts = P.dsem("st_s")
    em = Em(P)
    sem_of = {"mm": mm, "ev_s": evs}
    bank_free = [("mm", 0)] * 6
    wslot_cast = [0, 0]
    wbf_free = [0, 0]
    vst_cast = [0]
    vbf_free = [0, 0]
    ot_free = [0] * 4
    xn_free = 0

    for s in range(NSEQ):
        def ldseq(q, s=s, gate=xn_free):
            if gate:
                q.wait_ge(mm.h, gate)
            q.dma_start(out=_t3(xnb, NDC, L), in_=_fm_in(P.dram["XN1"][s])).then_inc(ldx.h, 16)
        ldx_v = ldx.inc(16)
        em.on("sync", ldseq)

        jobs = []
        for hc in range(HEADS):
            jobs.append(("wqkv_t", hc * 384, "Q1F", hc, AF.Silu))
        for hc in range(HEADS):
            jobs.append(("wqkv_t", hc * 384 + 128, "K1F", hc, AF.Silu))
        for oc in range(NDC):
            jobs.append(("wgate_t", oc * 128, "GF", oc, AF.Sigmoid))

        for ji, (wnm, col0, onm, oc, act) in enumerate(jobs):
            sl = ji % 2

            def wload(q, wnm=wnm, col0=col0, sl=sl, gate=wslot_cast[sl]):
                if gate:
                    q.wait_ge(cp.h, gate)
                q.dma_start(out=_t3(wst[sl], NDC, 128),
                            in_=_fm_in(P.dram[wnm], col0, col0 + 128)).then_inc(ldw[sl].h, 16)
            ldw_v = ldw[sl].inc(16)
            em.on("sync", wload)

            def wcast(g, sl=sl, lv=ldw_v, gate=wbf_free[sl]):
                g.wait_ge(ldw[sl].h, lv)
                if gate:
                    g.wait_ge(mm.h, gate)
                g.tensor_copy(out=wbf[sl][:], in_=wst[sl][:]).then_inc(cp.h, 1)
            wslot_cast[sl] = cp.inc()
            cast_v = cp.n
            em.on("gpsimd", wcast)

            for tb in range(4):
                b = tb % 3
                def qmm(te, b=b, sl=sl, tb=tb, cv=cast_v, lv=ldx_v, fr=bank_free[b]):
                    te.wait_ge(cp.h, cv)
                    te.wait_ge(ldx.h, lv)
                    nm_, v_ = fr
                    if v_:
                        te.wait_ge(sem_of[nm_].h, v_)
                    for c in range(NDC):
                        ins = te.matmul(pb[b][:], wbf[sl][:, c * 128:(c + 1) * 128],
                                        xnb[:, c * L + tb * TB: c * L + (tb + 1) * TB],
                                        start=(c == 0), stop=(c == NDC - 1))
                    ins.then_inc(mm.h, 1)
                mm_v = mm.inc()
                em.on("tensor", qmm)

                osl = tb
                def qevac(sc, b=b, osl=osl, act=act, mv=mm_v, fr=ot_free[osl]):
                    sc.wait_ge(mm.h, mv)
                    if fr:
                        sc.wait_ge(sts.h, fr)
                    sc.activation(out=otile[osl][:], in_=pb[b][:], func=act
                                  ).then_inc(evs.h, 1)
                ev_val = evs.inc()
                bank_free[b] = ("ev_s", ev_val)
                em.on("scalar", qevac)

                def qstore(sc, osl=osl, onm=onm, s=s, oc=oc, tb=tb, ev=ev_val):
                    sc.wait_ge(evs.h, ev)
                    sc.dma_start(out=P.dram[onm][s, oc * 128:(oc + 1) * 128,
                                                 tb * TB:(tb + 1) * TB],
                                 in_=otile[osl][:]).then_inc(sts.h, 16)
                ot_free[osl] = sts.inc(16)
                em.on("scalar", qstore)
            wbf_free[sl] = mm.n

        for g4 in range(4):
            vsl = g4 % 2

            def vload(q, g4=g4, gate=vst_cast[0]):
                if gate:
                    q.wait_ge(cp.h, gate)
                vst4 = vst[:].rearrange("p (c h e) -> p h c e", c=NDC, h=4)
                for h4 in range(4):
                    h = g4 * 4 + h4
                    q.dma_start(out=vst4[:, h4],
                                in_=_fm_in(P.dram["wqkv_t"],
                                           h * 384 + 256, h * 384 + 384)
                                ).then_inc(ldv.h, 16)
            ldv_v = ldv.inc(64)
            em.on("sync", vload)

            def vcast(g, lv=ldv_v, vsl=vsl, gate=vbf_free[vsl]):
                g.wait_ge(ldv.h, lv)
                if gate:
                    g.wait_ge(mm.h, gate)
                g.tensor_copy(out=vbf[vsl][:], in_=vst[:]).then_inc(cp.h, 1)
            cast_v = cp.inc()
            vst_cast[0] = cast_v
            em.on("gpsimd", vcast)

            for tc in range(NDC):
                b = 3 + tc % 3
                def vmm(te, b=b, vsl=vsl, tc=tc, cv=cast_v, fr=bank_free[b]):
                    te.wait_ge(cp.h, cv)
                    nm_, v_ = fr
                    if v_:
                        te.wait_ge(sem_of[nm_].h, v_)
                    for c in range(NDC):
                        ins = te.matmul(pb[b][:],
                                        xnb[:, c * L + tc * 128: c * L + tc * 128 + 128],
                                        vbf[vsl][:, c * TB:(c + 1) * TB],
                                        start=(c == 0), stop=(c == NDC - 1))
                    ins.then_inc(mm.h, 1)
                mm_v = mm.inc()
                em.on("tensor", vmm)

                vtl = tc % 4
                def vevac(sc, b=b, vtl=vtl, mv=mm_v, fr=ot_free[vtl]):
                    sc.wait_ge(mm.h, mv)
                    if fr:
                        sc.wait_ge(sts.h, fr)
                    sc.activation(out=otile[vtl][:], in_=pb[b][:], func=AF.Silu
                                  ).then_inc(evs.h, 1)
                ev_val = evs.inc()
                bank_free[b] = ("ev_s", ev_val)
                em.on("scalar", vevac)

                def vstore(sc, vtl=vtl, s=s, tc=tc, g4=g4, ev=ev_val):
                    sc.wait_ge(evs.h, ev)
                    sc.dma_start(out=P.dram["V1T"][s, tc * 128:(tc + 1) * 128,
                                                   g4 * TB:(g4 + 1) * TB],
                                 in_=otile[vtl][:]).then_inc(sts.h, 16)
                ot_free[vtl] = sts.inc(16)
                em.on("scalar", vstore)
            vbf_free[vsl] = mm.n
        xn_free = mm.n

    em.flush()
    es.close()
    P.barrier()


def ph_lit(P):
    nc = P.nc
    es = ExitStack()
    t = P.t
    masks = es.enter_context(nc.sbuf_tensor("li_mask", [128, HEADS * 2 * BLK], F32))
    qdb = es.enter_context(nc.sbuf_tensor("li_qdb", [128, HEADS * BLK], BF16))
    kd_sb = es.enter_context(nc.sbuf_tensor("li_kd", [128, 2 * HEADS], F32))
    bd_sb = es.enter_context(nc.sbuf_tensor("li_bd", [128, HEADS], F32))
    idx = es.enter_context(nc.sbuf_tensor("li_idx", [128, 2 * BLK], F32))
    idxc = es.enter_context(nc.sbuf_tensor("li_idxc", [128, 2 * BLK], F32))
    b01 = es.enter_context(nc.sbuf_tensor("li_b01", [128, 2 * BLK], F32))
    idxk = es.enter_context(nc.sbuf_tensor("li_idxk", [128, 2], F32))
    arr_row = es.enter_context(nc.sbuf_tensor("li_arr", [1, BLK], F32))
    e_t = es.enter_context(nc.sbuf_tensor("li_e", [128, 2 * BLK], F32))
    qd_row = es.enter_context(nc.sbuf_tensor("li_qdr", [1, BLK], BF16))
    kvf = es.enter_context(nc.sbuf_tensor("li_kvf", [128, HEADS * 128], F32))
    kvb = es.enter_context(nc.sbuf_tensor("li_kvb", [128, HEADS * 128], BF16))
    qb_sb = [es.enter_context(nc.sbuf_tensor(f"li_qb{i}", [128, NDC * BLK], BF16)) for i in range(2)]
    kb_sb = [es.enter_context(nc.sbuf_tensor(f"li_kb{i}", [128, NDC * BLK], BF16)) for i in range(2)]
    vb_sb = [es.enter_context(nc.sbuf_tensor(f"li_vb{i}", [128, 2 * D], BF16)) for i in range(2)]
    qis = [es.enter_context(nc.sbuf_tensor(f"li_qis{i}", [128, BLK], BF16)) for i in range(2)]
    qkm = [es.enter_context(nc.sbuf_tensor(f"li_qkm{i}", [128, 2 * BLK], BF16)) for i in range(2)]
    kis = [es.enter_context(nc.sbuf_tensor(f"li_kis{i}", [128, 2 * 128], BF16)) for i in range(2)]
    lo_sb = [es.enter_context(nc.sbuf_tensor(f"li_lo{i}", [128, HEADS * BLK], BF16)) for i in range(2)]
    p_o = [es.enter_context(nc.psum_tensor(f"li_po{i}", [128, BLK], F32)) for i in range(2)]
    p_qk = [es.enter_context(nc.psum_tensor(f"li_pqk{i}", [128, 2 * BLK], F32)) for i in range(2)]
    p_tr = [es.enter_context(nc.psum_tensor(f"li_ptr{i}", [128, 2 * 128], BF16)) for i in range(2)]
    p_kv = [es.enter_context(nc.psum_tensor(f"li_pkv{i}", [128, 128], F32)) for i in range(2)]

    ldq = [P.dsem("ld_w0"), P.dsem("ld_w1")]
    mm = P.sem("mm")
    evv = P.sem("ev_v")
    evs = P.sem("ev_s")
    cp = P.sem("cast_p")
    sts = P.dsem("st_s")
    em = Em(P)

    def pool_pre(g):
        g.iota(idx[:, 0:BLK], [[1, BLK]], base=0, channel_multiplier=-1,
               allow_small_or_imprecise_dtypes=True)
        g.iota(idx[:, BLK:2 * BLK], [[1, BLK]], base=-128, channel_multiplier=-1,
               allow_small_or_imprecise_dtypes=True)
        g.iota(arr_row[:], [[1, BLK]], base=1, channel_multiplier=0,
               allow_small_or_imprecise_dtypes=True)
        g.iota(idxk[:, 0:1], [[1, 1]], base=255, channel_multiplier=-1,
               allow_small_or_imprecise_dtypes=True)
        g.iota(idxk[:, 1:2], [[1, 1]], base=127, channel_multiplier=-1,
               allow_small_or_imprecise_dtypes=True).then_inc(cp.h, 1)
    pre_v = cp.inc()
    em.on("gpsimd", pool_pre)

    def dve_pre(v, pv=pre_v):
        v.wait_ge(cp.h, pv)
        v.tensor_scalar(out=idxc[:], in0=idx[:], scalar1=0.0, scalar2=None, op0=OP.max)
        v.tensor_scalar(out=b01[:], in0=idx[:], scalar1=0.0, scalar2=None,
                        op0=OP.is_ge).then_inc(evv.h, 1)
    dvepre_v = evv.inc()
    em.on("vector", dve_pre)

    mask_done = []
    qd_psum_free = 0
    prev_mask_v = 0
    prev_bc_v = 0
    for h in range(HEADS):
        def acts(sc, h=h, dv=dvepre_v, pm=prev_mask_v, pq=prev_bc_v):
            if h == 0:
                sc.wait_ge(evv.h, dv)
            if pm:
                sc.wait_ge(evv.h, pm)
            if pq:
                sc.wait_ge(mm.h, pq)
            sc.activation(out=e_t[:], in_=idxc[:], func=AF.Exp,
                          scale=t["slope_nb"][:, h:h + 1])
            sc.activation(out=kd_sb[:, h:h + 1], in_=idxk[:, 0:1], func=AF.Exp,
                          scale=t["slope_nb"][:, h:h + 1])
            sc.activation(out=kd_sb[:, HEADS + h:HEADS + h + 1], in_=idxk[:, 1:2],
                          func=AF.Exp, scale=t["slope_nb"][:, h:h + 1])
            sc.activation(out=bd_sb[:, h:h + 1], in_=t["slope_b"][:, h:h + 1],
                          func=AF.Exp, scale=-256.0)
            sc.activation(out=qd_row[:], in_=arr_row[:], func=AF.Exp,
                          scale=t["slope_nb"][0:1, h:h + 1]).then_inc(evs.h, 1)
        act_v = evs.inc()
        em.on("scalar", acts)

        def dves(v, h=h, av=act_v):
            v.wait_ge(evs.h, av)
            v.tensor_tensor(out=masks[:, h * 2 * BLK:(h * 2 + 2) * BLK], in0=e_t[:],
                            in1=b01[:], op=OP.mult).then_inc(evv.h, 1)
        dve_v = evv.inc()
        mask_done.append(dve_v)
        em.on("vector", dves)

        def bcmm(te, h=h, av=act_v, dv=dve_v, fr=qd_psum_free):
            te.wait_ge(evs.h, av)
            te.wait_ge(evv.h, dv)
            if fr:
                te.wait_ge(evv.h, fr)
            te.matmul(p_qk[0][:, 0:BLK], t["ones_row"][:], qd_row[:],
                      start=True, stop=True).then_inc(mm.h, 1)
        bc_v = mm.inc()
        em.on("tensor", bcmm)

        def qevac(v, h=h, bv=bc_v):
            v.wait_ge(mm.h, bv)
            v.tensor_copy(out=qdb[:, h * BLK:(h + 1) * BLK], in_=p_qk[0][:, 0:BLK]
                          ).then_inc(evv.h, 1)
        qd_psum_free = evv.inc()
        em.on("vector", qevac)
        prev_mask_v = dve_v
        prev_bc_v = bc_v
    pre_done_mm = mm.n
    pre_done_evv = evv.n

    qis_free = [0, 0]
    qkm_free = [0, 0]
    kis_free = [0, 0]
    pqk_free = [0, 0]
    ptr_free = [0, 0]
    po_free = [0, 0]
    pkv_free = [0, 0]
    lo_free = [0, 0]
    slot_mm_free = [0, 0]
    kv_cast = [[0] * HEADS, [0] * HEADS]
    kvb_cast_v = [0] * HEADS
    kvupd_v = [0] * HEADS
    pqk_free[0] = pre_done_evv

    seq_mm_done = 0
    for s in range(NSEQ):
        def zero(v, s=s, gate=seq_mm_done):
            if gate:
                v.wait_ge(mm.h, gate)
            v.memset(kvf[:], 0.0).then_inc(evv.h, 1)
        z_v = evv.inc()
        em.on("vector", zero)

        def zerob(g, zv=z_v, gate=seq_mm_done):
            if gate:
                g.wait_ge(mm.h, gate)
            g.memset(kvb[:], 0.0).then_inc(cp.h, 1)
        zb_v = cp.inc()
        em.on("gpsimd", zerob)
        kvb_cast_v = [zb_v] * HEADS
        kvupd_v = [z_v] * HEADS

        for i in range(NBLK):
            sl = i % 2
            ip = i % 2

            def load(q, s=s, i=i, sl=sl, gate=slot_mm_free[sl]):
                if gate:
                    q.wait_ge(mm.h, gate)

                q.dma_start(out=_t3(qb_sb[sl], NDC, BLK),
                            in_=_fm_in(P.dram["Q1F"][s], i * BLK, (i + 1) * BLK)
                            ).then_inc(ldq[sl].h, 16)
                q.dma_start(out=_t3(kb_sb[sl], NDC, BLK),
                            in_=_fm_in(P.dram["K1F"][s], i * BLK, (i + 1) * BLK)
                            ).then_inc(ldq[sl].h, 16)
                q.dma_start(out=_t3(vb_sb[sl], 2, D),
                            in_=P.dram["V1T"][s][i * BLK:(i + 1) * BLK, :]
                            .rearrange("(tc p) e -> p tc e", p=128)).then_inc(ldq[sl].h, 16)
            ld_v = ldq[sl].inc(48)
            em.on("sync", load)

            for h in range(HEADS):
                hp = h % 2

                def pqis(g, sl=sl, h=h, hp=hp, lv=ld_v, fr=qis_free[hp], pe=pre_done_evv):
                    g.wait_ge(ldq[sl].h, lv)
                    g.wait_ge(evv.h, pe)
                    if fr:
                        g.wait_ge(mm.h, fr)
                    g.tensor_tensor(out=qis[hp][:], in0=qb_sb[sl][:, h * BLK:(h + 1) * BLK],
                                    in1=qdb[:, h * BLK:(h + 1) * BLK], op=OP.mult
                                    ).then_inc(cp.h, 1)
                qis_v = cp.inc()
                em.on("gpsimd", pqis)

                def pqk_mm(te, sl=sl, h=h, hp=hp, lv=ld_v, fr=pqk_free[hp], fr2=ptr_free[hp]):
                    te.wait_ge(ldq[sl].h, lv)
                    if fr:
                        te.wait_ge(evv.h, fr)
                    for nc2 in range(2):
                        ins = te.matmul(p_qk[hp][:, nc2 * BLK:(nc2 + 1) * BLK],
                                        kb_sb[sl][:, h * BLK + nc2 * 128: h * BLK + nc2 * 128 + 128],
                                        qb_sb[sl][:, h * BLK:(h + 1) * BLK],
                                        start=True, stop=True)
                    ins.then_inc(mm.h, 1)
                    if fr2:
                        te.wait_ge(evv.h, fr2)
                    for nc2 in range(2):
                        ins = te.transpose(p_tr[hp][:, nc2 * 128:(nc2 + 1) * 128],
                                           kb_sb[sl][:, h * BLK + nc2 * 128: h * BLK + nc2 * 128 + 128],
                                           t["ident"][:])
                    ins.then_inc(mm.h, 1)
                qk_v = mm.inc()
                tr_v = mm.inc()
                em.on("tensor", pqk_mm)

                def dqkm(v, h=h, hp=hp, qv=qk_v, tv=tr_v, fr=qkm_free[hp], fr2=kis_free[hp]):
                    v.wait_ge(mm.h, qv)
                    if fr:
                        v.wait_ge(mm.h, fr)
                    v.tensor_tensor(out=qkm[hp][:], in0=p_qk[hp][:],
                                    in1=masks[:, h * 2 * BLK:(h * 2 + 2) * BLK],
                                    op=OP.mult).then_inc(evv.h, 1)
                    v.wait_ge(mm.h, tv)
                    if fr2:
                        v.wait_ge(mm.h, fr2)
                    for nc2 in range(2):
                        ins = v.tensor_scalar(
                            out=kis[hp][:, nc2 * 128:(nc2 + 1) * 128],
                            in0=p_tr[hp][:, nc2 * 128:(nc2 + 1) * 128],
                            scalar1=kd_sb[:, nc2 * HEADS + h:nc2 * HEADS + h + 1],
                            scalar2=None, op0=OP.mult)
                    ins.then_inc(evv.h, 1)
                qkm_v = evv.inc()
                kis_v = evv.inc()
                pqk_free[hp] = qkm_v
                ptr_free[hp] = kis_v
                em.on("vector", dqkm)

                def pom(te, sl=sl, h=h, hp=hp, qv=qis_v, kv_c=kvb_cast_v[h],
                        qm=qkm_v, kv2=kis_v, fro=po_free[hp], frk=pkv_free[hp]):
                    te.wait_ge(cp.h, max(qv, kv_c))
                    te.wait_ge(evv.h, qm)
                    if fro:
                        te.wait_ge(evs.h, fro)
                    te.matmul(p_o[hp][:], kvb[:, h * 128:(h + 1) * 128], qis[hp][:],
                              start=True, stop=False)
                    for nc2 in range(2):
                        ins = te.matmul(p_o[hp][:],
                                        vb_sb[sl][:, nc2 * D + h * 128: nc2 * D + h * 128 + 128],
                                        qkm[hp][:, nc2 * BLK:(nc2 + 1) * BLK],
                                        start=False, stop=(nc2 == 1))
                    ins.then_inc(mm.h, 1)
                    te.wait_ge(evv.h, kv2)
                    if frk:
                        te.wait_ge(evv.h, frk)
                    for nc2 in range(2):
                        ins = te.matmul(p_kv[hp][:],
                                        kis[hp][:, nc2 * 128:(nc2 + 1) * 128],
                                        vb_sb[sl][:, nc2 * D + h * 128: nc2 * D + h * 128 + 128],
                                        start=(nc2 == 0), stop=(nc2 == 1))
                    ins.then_inc(mm.h, 1)
                o_v = mm.inc()
                kvd_v = mm.inc()
                qis_free[hp] = o_v
                qkm_free[hp] = o_v
                kis_free[hp] = kvd_v
                em.on("tensor", pom)

                def dkv(v, h=h, hp=hp, kd_v=kvd_v, pu=kvupd_v[h]):
                    v.wait_ge(mm.h, kd_v)
                    v.scalar_tensor_tensor(out=kvf[:, h * 128:(h + 1) * 128],
                                           in0=kvf[:, h * 128:(h + 1) * 128],
                                           scalar=bd_sb[:, h:h + 1],
                                           in1=p_kv[hp][:], op0=OP.mult, op1=OP.add
                                           ).then_inc(evv.h, 1)
                upd_v = evv.inc()
                pkv_free[hp] = upd_v
                kvupd_v[h] = upd_v
                em.on("vector", dkv)

                def pcast(g, h=h, uv=upd_v):
                    g.wait_ge(evv.h, uv)
                    g.tensor_copy(out=kvb[:, h * 128:(h + 1) * 128],
                                  in_=kvf[:, h * 128:(h + 1) * 128]).then_inc(cp.h, 1)
                kvb_cast_v[h] = cp.inc()
                em.on("gpsimd", pcast)

                def sevac(sc, h=h, hp=hp, sl=sl, ov=o_v, fr=lo_free[sl]):
                    sc.wait_ge(mm.h, ov)
                    if fr and h == 0:
                        sc.wait_ge(sts.h, fr)
                    sc.activation(out=lo_sb[sl][:, h * BLK:(h + 1) * BLK], in_=p_o[hp][:],
                                  func=AF.Copy).then_inc(evs.h, 1)
                ev_val = evs.inc()
                po_free[hp] = ev_val
                em.on("scalar", sevac)

            def lstore(sc, s=s, i=i, sl=sl, ev=evs.n):
                sc.wait_ge(evs.h, ev)
                sc.dma_start(out=_fm_in(P.dram["LO"][s], i * BLK, (i + 1) * BLK),
                             in_=_t3(lo_sb[sl], HEADS, BLK)).then_inc(sts.h, 16)
            lo_free[sl] = sts.inc(16)
            em.on("scalar", lstore)
            slot_mm_free[sl] = mm.n
        seq_mm_done = mm.n

    em.flush()
    es.close()
    P.barrier()


def ph_f1a(P):
    nc = P.nc
    es = ExitStack()
    xnb = es.enter_context(nc.sbuf_tensor("fa_xn", [128, NDC * L], BF16))
    gst = [es.enter_context(nc.sbuf_tensor(f"fa_gs{i}", [128, NDC * 128], F32)) for i in range(2)]
    ust = [es.enter_context(nc.sbuf_tensor(f"fa_us{i}", [128, NDC * 128], F32)) for i in range(2)]
    gbf = [es.enter_context(nc.sbuf_tensor(f"fa_gb{i}", [128, NDC * 128], BF16)) for i in range(2)]
    ubf = [es.enter_context(nc.sbuf_tensor(f"fa_ub{i}", [128, NDC * 128], BF16)) for i in range(2)]
    gsil = [es.enter_context(nc.sbuf_tensor(f"fa_gsil{i}", [128, TB], BF16)) for i in range(2)]
    hdn_t = [es.enter_context(nc.sbuf_tensor(f"fa_hd{i}", [128, 4 * TB], BF16)) for i in range(2)]
    pg = [es.enter_context(nc.psum_tensor(f"fa_pg{i}", [128, TB], F32)) for i in range(2)]
    pu = [es.enter_context(nc.psum_tensor(f"fa_pu{i}", [128, TB], F32)) for i in range(2)]

    ldx = P.dsem("ld_b")
    ldw = [P.dsem("ld_w0"), P.dsem("ld_w1")]
    cp = P.sem("cast_p")
    mm = P.sem("mm")
    evv = P.sem("ev_v")
    evs = P.sem("ev_s")
    stg = P.dsem("st_g")
    t = P.t
    em = Em(P)
    pg_free = [0, 0]
    pu_free = [0, 0]
    gsil_free = [0, 0]
    wslot_cast = [0, 0]
    wbf_free = [0, 0]
    hdn_free = [0, 0]
    xn_free = 0

    for s in range(NSEQ):
        def ldseq(q, s=s, gate=xn_free):
            if gate:
                q.wait_ge(mm.h, gate)
            q.dma_start(out=_t3(xnb, NDC, L), in_=_fm_in(P.dram["XN1"][s])).then_inc(ldx.h, 16)
        ldx_v = ldx.inc(16)
        em.on("sync", ldseq)

        for ic in range(NIC):
            sl = (s * NIC + ic) % 2

            def wload(q, ic=ic, sl=sl, gate=wslot_cast[sl]):
                if gate:
                    q.wait_ge(cp.h, gate)
                q.dma_start(out=_t3(gst[sl], NDC, 128),
                            in_=_fm_in(P.dram["gw_t"], ic * 128, (ic + 1) * 128)
                            ).then_inc(ldw[sl].h, 16)
                q.dma_start(out=_t3(ust[sl], NDC, 128),
                            in_=_fm_in(P.dram["uw_t"], ic * 128, (ic + 1) * 128)
                            ).then_inc(ldw[sl].h, 16)
            ldw_v = ldw[sl].inc(32)
            em.on("sync", wload)

            def wcast(g, sl=sl, lv=ldw_v, gate=wbf_free[sl]):
                g.wait_ge(ldw[sl].h, lv)
                if gate:
                    g.wait_ge(mm.h, gate)
                g.tensor_copy(out=gbf[sl][:], in_=gst[sl][:])
                g.tensor_copy(out=ubf[sl][:], in_=ust[sl][:]).then_inc(cp.h, 1)
            wslot_cast[sl] = cp.inc()
            cast_v = cp.n
            em.on("gpsimd", wcast)

            for tb in range(4):
                bp = tb % 2

                def gmm(te, bp=bp, sl=sl, tb=tb, cv=cast_v, lv=ldx_v,
                        frg=pg_free[bp], fru=pu_free[bp]):
                    te.wait_ge(cp.h, cv)
                    te.wait_ge(ldx.h, lv)
                    if frg:
                        te.wait_ge(evs.h, frg)
                    for c in range(NDC):
                        ins = te.matmul(pg[bp][:], gbf[sl][:, c * 128:(c + 1) * 128],
                                        xnb[:, c * L + tb * TB: c * L + (tb + 1) * TB],
                                        start=(c == 0), stop=(c == NDC - 1))
                    ins.then_inc(mm.h, 1)
                    if fru:
                        te.wait_ge(evv.h, fru)
                    for c in range(NDC):
                        ins = te.matmul(pu[bp][:], ubf[sl][:, c * 128:(c + 1) * 128],
                                        xnb[:, c * L + tb * TB: c * L + (tb + 1) * TB],
                                        start=(c == 0), stop=(c == NDC - 1))
                    ins.then_inc(mm.h, 1)
                g_v = mm.inc()
                u_v = mm.inc()
                em.on("tensor", gmm)

                def gev(sc, bp=bp, ic=ic, gv=g_v, fr=gsil_free[bp]):
                    sc.wait_ge(mm.h, gv)
                    if fr:
                        sc.wait_ge(evv.h, fr)
                    sc.activation(out=gsil[bp][:], in_=pg[bp][:], func=AF.Silu,
                                  scale=t["gs_c"][:, ic:ic + 1]).then_inc(evs.h, 1)
                gsil_v = evs.inc()
                pg_free[bp] = gsil_v
                em.on("scalar", gev)

                def hdnv(v, bp=bp, sl=sl, ic=ic, tb=tb, uv=u_v, gv=gsil_v,
                         fr=hdn_free[sl]):
                    v.wait_ge(mm.h, uv)
                    v.wait_ge(evs.h, gv)
                    if fr and tb == 0:
                        v.wait_ge(stg.h, fr)
                    v.scalar_tensor_tensor(out=hdn_t[sl][:, tb * TB:(tb + 1) * TB],
                                           in0=pu[bp][:], scalar=t["us_c"][:, ic:ic + 1],
                                           in1=gsil[bp][:], op0=OP.mult, op1=OP.mult
                                           ).then_inc(evv.h, 1)
                hv = evv.inc()
                pu_free[bp] = hv
                gsil_free[bp] = hv
                em.on("vector", hdnv)

            def hstore(g, s=s, ic=ic, sl=sl, ev=evv.n):
                g.wait_ge(evv.h, ev)
                g.dma_start(out=P.dram["HDN"][ic * 128:(ic + 1) * 128,
                                             s * L:(s + 1) * L],
                            in_=hdn_t[sl][:]).then_inc(stg.h, 16)
            hdn_free[sl] = stg.inc(16)
            em.on("gpsimd", hstore)
            wbf_free[sl] = mm.n
        xn_free = mm.n

    em.flush()
    es.close()
    P.barrier()


def ph_f1b(P):
    nc = P.nc
    es = ExitStack()
    dwst = [es.enter_context(nc.sbuf_tensor(f"fb_ds{i}", [128, 1024], F32)) for i in range(2)]
    dwb = es.enter_context(nc.sbuf_tensor("fb_dwb", [128, NIC * 1024], BF16))
    hdn_t = [es.enter_context(nc.sbuf_tensor(f"fb_hd{i}", [128, NIC * TB], BF16)) for i in range(2)]
    g1_t = [es.enter_context(nc.sbuf_tensor(f"fb_g1{i}", [128, TB], BF16)) for i in range(4)]
    pb = [es.enter_context(nc.psum_tensor(f"fb_p{i}", [128, TB], F32)) for i in range(3)]

    ldw = [P.dsem("ld_w0"), P.dsem("ld_w1")]
    ldh = [P.dsem("ld_b"), P.dsem("ld_c")]
    cp = P.sem("cast_p")
    mm = P.sem("mm")
    evs = P.sem("ev_s")
    sts = P.dsem("st_s")
    t = P.t
    em = Em(P)
    bank_free = [0, 0, 0]
    g1_free = [0] * 4
    hdn_mm_free = [0, 0]
    dw_stage_cast = [0, 0]
    dwb_mm_free = 0

    for hg in range(2):
        for ic in range(NIC):
            sl = ic % 2

            def dload(q, hg=hg, ic=ic, sl=sl, gate=dw_stage_cast[sl]):
                if gate:
                    q.wait_ge(cp.h, gate)
                q.dma_start(out=dwst[sl][:],
                            in_=P.dram["dw_t"][ic * 128:(ic + 1) * 128,
                                               hg * 1024:(hg + 1) * 1024]
                            ).then_inc(ldw[sl].h, 16)
            ldw_v = ldw[sl].inc(16)
            em.on("sync", dload)

            def dcast(g, ic=ic, sl=sl, lv=ldw_v, gate=dwb_mm_free):
                g.wait_ge(ldw[sl].h, lv)
                if gate and ic == 0:
                    g.wait_ge(mm.h, gate)
                g.tensor_copy(out=dwb[:, ic * 1024:(ic + 1) * 1024], in_=dwst[sl][:]
                              ).then_inc(cp.h, 1)
            dw_stage_cast[sl] = cp.inc()
            em.on("gpsimd", dcast)
        cast_all = cp.n

        for tb in range(8):
            s, tbs = divmod(tb, 4)
            sl = tb % 2

            def hload(q, s=s, tbs=tbs, sl=sl, gate=hdn_mm_free[sl]):
                if gate:
                    q.wait_ge(mm.h, gate)
                q.dma_start(out=_t3(hdn_t[sl], NIC, TB),
                            in_=P.dram["HDN"][:, s * L + tbs * TB: s * L + (tbs + 1) * TB]
                            .rearrange("(c p) w -> p c w", p=128)).then_inc(ldh[sl].h, 16)
            ldh_v = ldh[sl].inc(16)
            em.on("sync", hload)

            for hc in range(8):
                b = hc % 3

                def dmm(te, b=b, sl=sl, hc=hc, cv=cast_all, lv=ldh_v, fr=bank_free[b]):
                    te.wait_ge(cp.h, cv)
                    te.wait_ge(ldh[sl].h, lv)
                    if fr:
                        te.wait_ge(evs.h, fr)
                    for ic in range(NIC):
                        ins = te.matmul(pb[b][:],
                                        dwb[:, ic * 1024 + hc * 128: ic * 1024 + hc * 128 + 128],
                                        hdn_t[sl][:, ic * TB:(ic + 1) * TB],
                                        start=(ic == 0), stop=(ic == NIC - 1))
                    ins.then_inc(mm.h, 1)
                mm_v = mm.inc()
                em.on("tensor", dmm)

                gt = hc % 4
                def g1ev(sc, b=b, gt=gt, hg=hg, hc=hc, mv=mm_v, fr=g1_free[gt]):
                    sc.wait_ge(mm.h, mv)
                    if fr:
                        sc.wait_ge(sts.h, fr)
                    sc.activation(out=g1_t[gt][:], in_=pb[b][:], func=AF.Silu,
                                  scale=t["ds_c"][:, hg * 8 + hc:hg * 8 + hc + 1]
                                  ).then_inc(evs.h, 1)
                ev_val = evs.inc()
                bank_free[b] = ev_val
                em.on("scalar", g1ev)

                def g1store(sc, gt=gt, hg=hg, hc=hc, tb=tb, ev=ev_val):
                    sc.wait_ge(evs.h, ev)
                    sc.dma_start(out=P.dram["G1"][(hg * 8 + hc) * 128:(hg * 8 + hc + 1) * 128,
                                                  tb * TB:(tb + 1) * TB],
                                 in_=g1_t[gt][:]).then_inc(sts.h, 16)
                g1_free[gt] = sts.inc(16)
                em.on("scalar", g1store)
            hdn_mm_free[sl] = mm.n
        dwb_mm_free = mm.n

    em.flush()
    es.close()
    P.barrier()


def ph_wout(P):
    nc = P.nc
    es = ExitStack()
    p_sb = es.enter_context(nc.sbuf_tensor("wo_p", [128, NDC * L], BF16))
    lo_t = [es.enter_context(nc.sbuf_tensor(f"wo_lo{i}", [128, NDC * TB], BF16)) for i in range(2)]
    gf_t = [es.enter_context(nc.sbuf_tensor(f"wo_gf{i}", [128, NDC * TB], BF16)) for i in range(2)]
    sq_t = [es.enter_context(nc.sbuf_tensor(f"wo_sq{i}", [128, TB], BF16)) for i in range(2)]
    rstd_bf = es.enter_context(nc.sbuf_tensor("wo_rs", [1, TB], BF16))
    rb_sb = es.enter_context(nc.sbuf_tensor("wo_rb", [128, TB], F32))
    t2 = es.enter_context(nc.sbuf_tensor("wo_t2", [128, TB], F32))
    wst = [es.enter_context(nc.sbuf_tensor(f"wo_ws{i}", [128, NDC * 128], F32)) for i in range(2)]
    wbf = [es.enter_context(nc.sbuf_tensor(f"wo_wb{i}", [128, NDC * 128], BF16)) for i in range(2)]
    s1_t = [es.enter_context(nc.sbuf_tensor(f"wo_s1{i}", [128, TB], F32)) for i in range(2)]
    x1_t = [es.enter_context(nc.sbuf_tensor(f"wo_x1{i}", [128, TB], F32)) for i in range(2)]
    g1_t = [es.enter_context(nc.sbuf_tensor(f"wo_g1{i}", [128, TB], BF16)) for i in range(2)]
    t3_ = es.enter_context(nc.sbuf_tensor("wo_t3", [128, TB], F32))
    y_t = [es.enter_context(nc.sbuf_tensor(f"wo_y{i}", [128, TB], F32)) for i in range(2)]
    psum_r = [es.enter_context(nc.psum_tensor(f"wo_pr{i}", [128, TB], F32)) for i in range(2)]
    pbc = [es.enter_context(nc.psum_tensor(f"wo_pb{i}", [128, TB], F32)) for i in range(2)]
    pb = [es.enter_context(nc.psum_tensor(f"wo_pm{i}", [128, TB], F32)) for i in range(4)]

    ldl = [P.dsem("ld_a0"), P.dsem("ld_a1")]
    ldw = [P.dsem("ld_w0"), P.dsem("ld_w1")]
    ldr = [P.dsem("ld_b"), P.dsem("ld_c")]
    cp = P.sem("cast_p")
    mm = P.sem("mm")
    evv = P.sem("ev_v")
    evs = P.sem("ev_s")
    stg = P.dsem("st_g")
    t = P.t
    em = Em(P)
    lo_dve_free = [0, 0]
    sq_mm_free = [0, 0]
    pr_free = [0, 0]
    pbc_free = [0, 0]
    wslot_cast = [0, 0]
    wbf_free = [0, 0]
    bank_free = [0] * 4
    s1_free = [0, 0]
    xg_free = [0, 0]
    y_free = [0, 0]
    p_mm_free = 0

    for s in range(NSEQ):
        for tb in range(4):
            sl = tb % 2

            def aload(q, s=s, tb=tb, sl=sl, gate=lo_dve_free[sl]):
                if gate:
                    q.wait_ge(evv.h, gate)
                q.dma_start(out=_t3(lo_t[sl], NDC, TB),
                            in_=_fm_in(P.dram["LO"][s], tb * TB, (tb + 1) * TB)
                            ).then_inc(ldl[sl].h, 16)
                q.dma_start(out=_t3(gf_t[sl], NDC, TB),
                            in_=_fm_in(P.dram["GF"][s], tb * TB, (tb + 1) * TB)
                            ).then_inc(ldl[sl].h, 16)
            ldl_v = ldl[sl].inc(32)
            em.on("sync", aload)

            sq_vals = []
            for c in range(NDC):
                qsl = c % 2

                def sqa(sc, sl=sl, qsl=qsl, c=c, lv=ldl_v, fr=sq_mm_free[qsl]):
                    if c < 2:
                        sc.wait_ge(ldl[sl].h, lv)
                    if fr:
                        sc.wait_ge(mm.h, fr)
                    sc.activation(out=sq_t[qsl][:], in_=lo_t[sl][:, c * TB:(c + 1) * TB],
                                  func=AF.Square).then_inc(evs.h, 1)
                sq_v = evs.inc()
                em.on("scalar", sqa)

                def smm(te, qsl=qsl, sl=sl, c=c, sv=sq_v, fr=pr_free[sl]):
                    te.wait_ge(evs.h, sv)
                    if c == 0 and fr:
                        te.wait_ge(evs.h, fr)
                    te.matmul(psum_r[sl][0:1, :], P.t["ones_col"][:], sq_t[qsl][:],
                              start=(c == 0), stop=(c == NDC - 1)).then_inc(mm.h, 1)
                sq_mm_free[qsl] = mm.inc()
                em.on("tensor", smm)
            smm_last = mm.n

            def rstd(sc, sl=sl, mv=smm_last):
                sc.wait_ge(mm.h, mv)
                sc.activation(out=rstd_bf[:], in_=psum_r[sl][0:1, :], func=AF.Ln,
                              bias=t["eps1"][0:1, 0:1], scale=1.0 / D)
                sc.activation(out=rstd_bf[:], in_=rstd_bf[:], func=AF.Exp, scale=-0.5
                              ).then_inc(evs.h, 1)
            rstd_v = evs.inc()
            pr_free[sl] = rstd_v
            em.on("scalar", rstd)

            def bc(te, sl=sl, rv=rstd_v, fr=pbc_free[sl]):
                te.wait_ge(evs.h, rv)
                if fr:
                    te.wait_ge(evv.h, fr)
                te.matmul(pbc[sl][:], P.t["ones_row"][:], rstd_bf[:],
                          start=True, stop=True).then_inc(mm.h, 1)
            bc_v = mm.inc()
            em.on("tensor", bc)

            def ptiles(v, s=s, sl=sl, tb=tb, bv=bc_v, gate=p_mm_free):
                v.wait_ge(mm.h, bv)
                if gate and tb == 0:
                    v.wait_ge(mm.h, gate)
                v.tensor_copy(out=rb_sb[:], in_=pbc[sl][:])
                for c in range(NDC):
                    v.scalar_tensor_tensor(out=t2[:], in0=lo_t[sl][:, c * TB:(c + 1) * TB],
                                           scalar=t["nw_c"][:, c:c + 1], in1=rb_sb[:],
                                           op0=OP.mult, op1=OP.mult)
                    ins = v.tensor_tensor(out=p_sb[:, c * L + tb * TB: c * L + (tb + 1) * TB],
                                          in0=t2[:], in1=gf_t[sl][:, c * TB:(c + 1) * TB],
                                          op=OP.mult)
                ins.then_inc(evv.h, 1)
            pt_v = evv.inc()
            pbc_free[sl] = pt_v
            lo_dve_free[sl] = pt_v
            em.on("vector", ptiles)
        p_ready = evv.n

        for fc in range(NDC):
            sl = (s * NDC + fc) % 2

            def wload(q, fc=fc, sl=sl, gate=wslot_cast[sl]):
                if gate:
                    q.wait_ge(cp.h, gate)
                q.dma_start(out=_t3(wst[sl], NDC, 128),
                            in_=_fm_in(P.dram["wout_t"], fc * 128, (fc + 1) * 128)
                            ).then_inc(ldw[sl].h, 16)
            ldw_v = ldw[sl].inc(16)
            em.on("sync", wload)

            def wcast(g, sl=sl, lv=ldw_v, gate=wbf_free[sl]):
                g.wait_ge(ldw[sl].h, lv)
                if gate:
                    g.wait_ge(mm.h, gate)
                g.tensor_copy(out=wbf[sl][:], in_=wst[sl][:]).then_inc(cp.h, 1)
            wslot_cast[sl] = cp.inc()
            cast_v = cp.n
            em.on("gpsimd", wcast)

            for tb in range(4):
                b = tb % 4
                tsl = tb % 2

                def omm(te, b=b, sl=sl, tb=tb, cv=cast_v, pr=p_ready, fr=bank_free[b]):
                    te.wait_ge(cp.h, cv)
                    te.wait_ge(evv.h, pr)
                    if fr:
                        te.wait_ge(evs.h, fr)
                    for c in range(NDC):
                        ins = te.matmul(pb[b][:], wbf[sl][:, c * 128:(c + 1) * 128],
                                        p_sb[:, c * L + tb * TB: c * L + (tb + 1) * TB],
                                        start=(c == 0), stop=(c == NDC - 1))
                    ins.then_inc(mm.h, 1)
                mm_v = mm.inc()
                em.on("tensor", omm)

                def s1ev(sc, b=b, tsl=tsl, mv=mm_v, fr=s1_free[tsl]):
                    sc.wait_ge(mm.h, mv)
                    if fr:
                        sc.wait_ge(evv.h, fr)
                    sc.activation(out=s1_t[tsl][:], in_=pb[b][:], func=AF.Copy
                                  ).then_inc(evs.h, 1)
                s1_v = evs.inc()
                bank_free[b] = s1_v
                em.on("scalar", s1ev)

                def rload(q, s=s, fc=fc, tb=tb, tsl=tsl, gate=xg_free[tsl]):
                    if gate:
                        q.wait_ge(evv.h, gate)
                    q.dma_start(out=x1_t[tsl][:],
                                in_=P.dram["X1"][s, fc * 128:(fc + 1) * 128,
                                                 tb * TB:(tb + 1) * TB]).then_inc(ldr[tsl].h, 16)
                    q.dma_start(out=g1_t[tsl][:],
                                in_=P.dram["G1"][fc * 128:(fc + 1) * 128,
                                                 s * L + tb * TB: s * L + (tb + 1) * TB]
                                ).then_inc(ldr[tsl].h, 16)
                ldr_v = ldr[tsl].inc(32)
                em.on("sync", rload)

                def resid(v, tsl=tsl, sv=s1_v, lv=ldr_v, gate=y_free[tsl]):
                    v.wait_ge(evs.h, sv)
                    v.wait_ge(ldr[tsl].h, lv)
                    if gate:
                        v.wait_ge(stg.h, gate)
                    v.tensor_tensor(out=t3_[:], in0=s1_t[tsl][:], in1=g1_t[tsl][:],
                                    op=OP.mult)
                    v.tensor_tensor(out=y_t[tsl][:], in0=t3_[:], in1=x1_t[tsl][:],
                                    op=OP.add).then_inc(evv.h, 1)
                r_v = evv.inc()
                s1_free[tsl] = r_v
                xg_free[tsl] = r_v
                em.on("vector", resid)

                def ystore(g, s=s, fc=fc, tb=tb, tsl=tsl, ev=r_v):
                    g.wait_ge(evv.h, ev)
                    g.dma_start(out=P.dram["y"][s, fc * 128:(fc + 1) * 128,
                                                tb * TB:(tb + 1) * TB],
                                in_=y_t[tsl][:]).then_inc(stg.h, 16)
                y_free[tsl] = stg.inc(16)
                em.on("gpsimd", ystore)
            wbf_free[sl] = mm.n
        p_mm_free = mm.n

    em.flush()
    es.close()
    P.barrier()



def _col(a, nch):
    return np.ascontiguousarray(np.asarray(a, np.float32).reshape(nch, 128).T)


def _host_prep(inputs):
    T = lambda a: np.ascontiguousarray(np.asarray(a, np.float32).T)
    common = {
        "wq_t": T(inputs["L0_attn_Wq"]), "wk_t": T(inputs["L0_attn_Wk"]),
        "wv_t": T(inputs["L0_attn_Wv"]), "wo_t": T(inputs["L0_attn_Wo"]),
        "bq_c": _col(inputs["L0_attn_bq"], 16), "bk_c": _col(inputs["L0_attn_bk"], 16),
        "bo_c": _col(inputs["L0_attn_bo"], 16),
        "bv_row": np.asarray(inputs["L0_attn_bv"], np.float32).reshape(1, 2048),
        "Bq_t": T(inputs["L0_attn_Bq"]), "Bk_t": T(inputs["L0_attn_Bk"]),
        "Bv_t": T(inputs["L0_attn_Bv"]), "Bo_t": T(inputs["L0_attn_Bo"]),
        "Aq_t": T(inputs["L0_attn_Aq"]), "Ak_t": T(inputs["L0_attn_Ak"]),
        "Av_t": T(inputs["L0_attn_Av"]), "Ao_t": T(inputs["L0_attn_Ao"]),
        "factors": np.ascontiguousarray(np.asarray(inputs["L0_ffn_factors"], np.float32)),
        "fb_c": _col(inputs["L0_ffn_bias"], 16),
        "slope_b": np.ascontiguousarray(np.broadcast_to(
            np.asarray(inputs["L1_attn_slope_rate"], np.float32).reshape(1, 16), (128, 16))),
        "wqkv_t": T(inputs["L1_attn_w_qkv"]),
        "wgate_t": T(inputs["L1_attn_w_gate"]),
        "wout_t": T(inputs["L1_attn_w_out"]),
        "nw_c": _col(inputs["L1_attn_norm_w"], 16),
        "gw_t": T(inputs["L1_ffn_gate_w"]), "uw_t": T(inputs["L1_ffn_up_w"]),
        "dw_t": T(inputs["L1_ffn_down_w"]),
        "gs_c": _col(inputs["L1_ffn_gate_s"], 44), "us_c": _col(inputs["L1_ffn_up_s"], 44),
        "ds_c": _col(inputs["L1_ffn_down_s"], 16),
    }
    x = np.asarray(inputs["L0_attn_x"], np.float32)
    x_fm = np.ascontiguousarray(x.transpose(0, 2, 1))
    shards = [x_fm[2 * c:2 * c + 2] for c in range(8)]
    return common, shards


_IN_SPECS = {
    "x": ([NSEQ, D, L], F32),
    "wq_t": ([D, D], F32), "wk_t": ([D, D], F32), "wv_t": ([D, D], F32), "wo_t": ([D, D], F32),
    "bq_c": ([128, 16], F32), "bk_c": ([128, 16], F32), "bo_c": ([128, 16], F32),
    "bv_row": ([1, D], F32),
    "Bq_t": ([D, 16], F32), "Bk_t": ([D, 16], F32), "Bv_t": ([D, 16], F32), "Bo_t": ([D, 16], F32),
    "Aq_t": ([16, D], F32), "Ak_t": ([16, D], F32), "Av_t": ([16, D], F32), "Ao_t": ([16, D], F32),
    "factors": ([8, D, D], F32), "fb_c": ([128, 16], F32),
    "slope_b": ([128, 16], F32),
    "wqkv_t": ([D, 3 * D], F32), "wgate_t": ([D, D], F32), "wout_t": ([D, D], F32),
    "nw_c": ([128, 16], F32),
    "gw_t": ([D, INTER], F32), "uw_t": ([D, INTER], F32), "dw_t": ([INTER, D], F32),
    "gs_c": ([128, 44], F32), "us_c": ([128, 44], F32), "ds_c": ([128, 16], F32),
}

_SCRATCH = {
    "XN": ([NSEQ, D, L], BF16), "FSUM": ([D, D], BF16),
    "QF": ([NSEQ, D, L], BF16), "KF": ([NSEQ, D, L], BF16), "VT": ([NSEQ, L, D], BF16),
    "AO": ([NSEQ, D, L], BF16), "S0": ([NSEQ, D, L], F32),
    "X1": ([NSEQ, D, L], F32), "XN1": ([NSEQ, D, L], BF16),
    "Q1F": ([NSEQ, D, L], BF16), "K1F": ([NSEQ, D, L], BF16), "V1T": ([NSEQ, L, D], BF16),
    "LO": ([NSEQ, D, L], BF16), "GF": ([NSEQ, D, L], BF16),
    "HDN": ([INTER, NT], BF16), "G1": ([D, NT], BF16),
}

PHASE_INPUTS = {
    "pre": ["bq_c", "bk_c", "bo_c", "fb_c", "nw_c", "ds_c", "slope_b", "gs_c", "us_c", "bv_row"],
    "ln0": ["x"],
    "qkv0": ["wq_t", "wk_t", "wv_t", "Bq_t", "Bk_t", "Bv_t", "Aq_t", "Ak_t", "Av_t"],
    "att0": ["factors"],
    "r0a": ["wo_t", "Bo_t", "Ao_t"],
    "r0b": [],
    "ln1": [],
    "qkv1": ["wqkv_t", "wgate_t"],
    "gate": ["wgate_t"],
    "lit": [],
    "f1a": ["gw_t", "uw_t"],
    "f1b": ["dw_t"],
    "wout": ["wout_t"],
}

ALL_PHASES = ("pre", "ln0", "qkv0", "att0", "r0a", "r0b", "ln1", "qkv1", "lit", "f1a", "f1b", "wout")


def used_inputs(phases=None):
    if phases is None:
        phases = ALL_PHASES
    used = []
    for ph in phases:
        key = "ln1" if ph == "ln1" else ph
        for nm in PHASE_INPUTS.get(key, []):
            if nm not in used:
                used.append(nm)
    return used


def build_program(phases=ALL_PHASES, external=(), take=()):
    nc = bass.Bass("TRN2", target_bir_lowering=False)
    es = ExitStack()
    used = set(used_inputs(phases))
    dram = {}
    for nm, (shape, dt) in _IN_SPECS.items():
        if nm in used:
            dram[nm] = nc.dram_tensor(nm, shape, dt, kind="ExternalInput").ap()
    for nm, (shape, dt) in _SCRATCH.items():
        if nm in external:
            kind = "ExternalOutput"
        elif nm in take:
            kind = "ExternalInput"
        else:
            kind = "Internal"
        dram[nm] = nc.dram_tensor(nm, shape, dt, kind=kind).ap()
    dram["y"] = nc.dram_tensor("y", [NSEQ, D, L], F32, kind="ExternalOutput").ap()

    block = es.enter_context(nc.Block())
    P = Prog(nc, es, block)
    P.dram = dram
    global _last_P
    _last_P = P

    for ph in phases:
        if ph == "pre":
            ph_pre(P, es)
        elif ph == "ln0":
            ph_ln(P, "x", "XN")
        elif ph == "ln1":
            ph_ln(P, "X1", "XN1")
        elif ph == "qkv0":
            ph_qkv0(P)
        elif ph == "att0":
            ph_att0(P)
        elif ph == "r0a":
            ph_r0a(P)
        elif ph == "r0b":
            ph_r0b(P)
        elif ph == "qkv1":
            ph_qkv1(P)
        elif ph == "lit":
            ph_lit(P)
        elif ph == "f1a":
            ph_f1a(P)
        elif ph == "f1b":
            ph_f1b(P)
        elif ph == "wout":
            ph_wout(P)
        else:
            raise ValueError(ph)

    es.close()
    return nc


def kernel(**inputs):
    common, shards = _host_prep(inputs)
    nc = build_program()
    used = used_inputs()
    in_maps = [{nm: (shards[c] if nm == "x" else common[nm]) for nm in used}
               for c in range(8)]
    res = run_bass_kernel_spmd(nc, in_maps, core_ids=list(range(8)))
    y = np.concatenate([r["y"] for r in res.results], axis=0)
    return np.ascontiguousarray(y.transpose(0, 2, 1))
